# version 02
# baseline (speedup 1.0000x reference)
import contextlib
import numpy as np
import ml_dtypes
import concourse.bass as bass
import concourse.mybir as mybir
from concourse.bass_utils import run_bass_kernel_spmd
from concourse.alu_op_type import AluOpType as OP

F32 = mybir.dt.float32
BF16 = mybir.dt.bfloat16
F16 = mybir.dt.float16
I32 = mybir.dt.int32
FP8 = mybir.dt.float8e4
AF = mybir.ActivationFunctionType
bfdt = ml_dtypes.bfloat16

T = 2048
D = 2048
NT = 16
ND = 16
E = 8
F = 8192
H = 16
KD = 128
I_FF = 5632

ENGINES = ("sync", "tensor", "vector", "scalar", "gpsimd")


class SemW:
    __slots__ = ("name", "h", "n")

    def __init__(self, name):
        self.name = name
        self.h = None
        self.n = 0

    def mark(self, k=1):
        self.n += k
        return self.n


class Prog:
    def __init__(self, nc):
        self.nc = nc
        self.q = {e: [] for e in ENGINES}
        self.sems = []
        self.trace = {}

    def sem(self, name):
        s = SemW(name)
        self.sems.append(s)
        return s

    def emit(self, eng, fn):
        self.q[eng].append(fn)

    def wait(self, eng, sem, val):
        if val > 0:
            self.emit(eng, lambda q, s=sem, v=val: q.wait_ge(s.h, v))
            self.trace.setdefault(eng, []).append(("w", sem, val, len(self.q[eng]) - 1))

    def op(self, eng, fn, sem=None, k=1):
        if sem is None:
            self.emit(eng, lambda q, f=fn: f(q))
            return None
        tgt = sem.mark(k)
        self.emit(eng, lambda q, f=fn, s=sem, kk=k: f(q).then_inc(s.h, kk))
        self.trace.setdefault(eng, []).append(("i", sem, k, len(self.q[eng]) - 1))
        return tgt

    def check_deadlock(self):
        cur = {}
        ptr = {e: 0 for e in ENGINES}
        progress = True
        while progress:
            progress = False
            for e in ENGINES:
                evs = self.trace.get(e, [])
                while ptr[e] < len(evs):
                    kind, sem, val, pos = evs[ptr[e]]
                    if kind == "i":
                        cur[sem.name] = cur.get(sem.name, 0) + val
                        ptr[e] += 1
                        progress = True
                    else:
                        if cur.get(sem.name, 0) >= val:
                            ptr[e] += 1
                            progress = True
                        else:
                            break
        stuck = {e: ptr[e] for e in ENGINES if ptr[e] < len(self.trace.get(e, []))}
        if stuck:
            for e, pp in stuck.items():
                kind, sem, val, pos = self.trace[e][pp]
                print(f"DEADLOCK {e}: event {pp}/{len(self.trace[e])} waits {sem.name} >= {val} (cur {cur.get(sem.name, 0)}, total marks {sem.n})")
            return False
        return True

    def dma(self, eng, out, in_, sem):
        self.wait(eng, sem, sem.n)
        return self.op(eng, lambda q, o=out, i=in_: q.dma_start(out=o, in_=i), sem, 16)

    def finalize(self, stack):
        for s in self.sems:
            s.h = stack.enter_context(self.nc.semaphore(s.name))
        blk = stack.enter_context(self.nc.Block())
        for eng in ENGINES:
            if not self.q[eng]:
                continue
            thunks = self.q[eng]

            def body(q, _thunks=thunks):
                for fn in _thunks:
                    fn(q)

            getattr(blk, eng)(body)


class TransposeMill:

    def __init__(self, p, banks, tag, dtype=BF16, cons_eng="scalar"):
        self.p = p
        self.banks = banks
        self.dtype = dtype
        self.cons = cons_eng
        self.s_grp = p.sem(tag + "_tg")
        self.s_done = p.sem(tag + "_te")
        self.g = 0

    def view(self, j):
        bank = self.banks[(j // 8) % 2]
        bv = bank[:].bitcast(self.dtype) if self.dtype != F32 else bank[:]
        slot = j % 8
        w = 128 if self.dtype != F32 else 64
        return bv[:, slot * 128:(slot + 1) * 128]

    def go(self, src_fn, evac_fn, n, ident):
        p = self.p
        assert self.dtype != F32 or True
        for j0 in range(0, n, 8):
            gk = self.g
            p.wait("tensor", self.s_done, 8 * (gk - 1))
            cnt = min(8, n - j0)
            for jj in range(cnt):
                j = j0 + jj
                pv = self.view_g(gk, jj)
                p.op("tensor",
                     lambda q, j=j, pv=pv, src_fn=src_fn, ident=ident:
                     q.transpose(pv, src_fn(j), ident),
                     self.s_grp if jj == cnt - 1 else None)
            p.wait(self.cons, self.s_grp, self.s_grp.n)
            for jj in range(cnt):
                j = j0 + jj
                pv = self.view_g(gk, jj)
                evac_fn(j, pv)
            self.g += 1

    def view_g(self, gk, slot):
        bank = self.banks[gk % 2]
        bv = bank[:].bitcast(self.dtype) if self.dtype != F32 else bank[:]
        return bv[:, slot * 128:(slot + 1) * 128]


def make_ln_bundle(p, st, tag):
    return (p.sem(tag + "x"), p.sem(tag + "v"), p.sem(tag + "vv"), p.sem(tag + "s"),
            TransposeMill(p, st["pb"][6:8], tag + "m", BF16, "scalar"))


def emit_ln_transpose(p, st, src_dram, dstT_tiles, tag, col_sel=None, post_tile=None,
                      tiles=None, bundle=None):
    if tiles is None:
        tiles = [(i, i) for i in range(NT)]
    if bundle is None:
        bundle = make_ln_bundle(p, st, tag)
    s_x, s_v, s_vv, s_s, mill = bundle
    bx, bv, bg = s_x.n, s_v.n, mill.s_grp.n
    xt = st["xtile"]
    xnb = st["xnbf"]
    stats, mv = st["stats"], st["mv"]
    negmean, veps, rstd = st["negmean"], st["veps"], st["rstd"]
    for i, (srow, dcol) in enumerate(tiles):
        b = i % 2
        if i >= 2:
            p.wait("sync", s_v, bv + (i - 2) * 2 + 2)
        p.dma("sync", xt[b][:], src_dram[srow * 128:(srow + 1) * 128, :], s_x)
        p.wait("vector", s_x, bx + 16 * (i + 1))
        for c in range(4):
            p.op("vector",
                 lambda q, b=b, c=c: q.bn_stats(out=stats[:, c * 6:(c + 1) * 6],
                                                in_=xt[b][:, c * 512:(c + 1) * 512]),
                 s_vv if c == 3 else None)
        p.wait("vector", s_vv, s_vv.n)
        p.op("vector", lambda q: q.bn_aggr(out=mv[:], in_=stats[:]), s_vv)
        p.wait("vector", s_vv, s_vv.n)
        p.op("vector", lambda q: q.tensor_scalar(out=negmean[:], in0=mv[:, 0:1],
                                                 scalar1=-1.0, scalar2=None, op0=OP.mult))
        p.op("vector", lambda q: q.tensor_scalar(out=veps[:], in0=mv[:, 1:2],
                                                 scalar1=1e-5, scalar2=None, op0=OP.add), s_v)
        p.wait("scalar", s_v, s_v.n)
        p.op("scalar", lambda q: q.activation(out=rstd[:], in_=veps[:], func=AF.Sqrt), s_s)
        p.wait("vector", s_s, s_s.n)
        p.op("vector", lambda q: q.reciprocal(out=rstd[:], in_=rstd[:]), s_vv)
        p.wait("vector", s_vv, s_vv.n)
        if i >= 2:
            p.wait("vector", mill.s_grp, bg + (i - 2) * 2 + 2)
        p.op("vector",
             lambda q, b=b: q.tensor_scalar(out=xnb[b][:], in0=xt[b][:],
                                            scalar1=negmean[:], scalar2=rstd[:],
                                            op0=OP.add, op1=OP.mult), s_v)
        if post_tile is not None:
            post_tile(p, i, b, xt)
        p.wait("tensor", s_v, s_v.n)

        def src_fn(dt, b=b):
            if col_sel is None:
                return xnb[b][:, dt * 128:(dt + 1) * 128]
            start, step = col_sel(dt)
            return xnb[b][:, start:start + 127 * step + 1:step]

        def evac_fn(dt, pv, dcol=dcol):
            p.op("scalar",
                 lambda q, dt=dt, pv=pv, dcol=dcol:
                 q.copy(out=dstT_tiles[dt][:, dcol * 128:(dcol + 1) * 128], in_=pv),
                 mill.s_done)

        mill.go(src_fn, evac_fn, ND, st["ident_bf"][:])
    return mill.s_done


def emit_p1(p, st, dr):
    s_ln = emit_ln_transpose(p, st, dr["x"], st["xn1T"], "p1a")
    ln_done = s_ln.n

    s_m = p.sem("p1m")
    s_bt = p.sem("p1bt")
    s_be = p.sem("p1be")
    s_bv = p.sem("p1bv")
    s_bs = p.sem("p1bs")
    p.dma("sync", st["A_sb"][:], dr["A"].rearrange("(dt p) n -> p dt n", p=128), s_m)
    p.dma("sync", st["B_sb"][:], dr["B"].rearrange("(dt p) n -> p dt n", p=128), s_m)
    p.dma("sync", st["ld16"][:], dr["log_delta"].rearrange("(a b) -> a b", b=128), s_m)
    p.dma("sync", st["cb16"][:], dr["C_b"].rearrange("(a b) -> a b", b=128), s_m)
    p.dma("sync", st["db16"][:], dr["D_b"].rearrange("(a b) -> a b", b=128), s_m)
    p.wait("tensor", s_m, s_m.n)
    pb4 = st["pb"][4]
    trips = [(st["ld16"], st["ldT"]), (st["cb16"], st["cbT"]), (st["db16"], st["dbT"])]
    for k, (src, dst) in enumerate(trips):
        p.op("tensor",
             lambda q, src=src, k=k: q.transpose(pb4[:, k * 16:(k + 1) * 16], src[:],
                                                 st["ident_f32"][0:16, 0:16]),
             s_bt)
    p.wait("scalar", s_bt, s_bt.n)
    for k, (src, dst) in enumerate(trips):
        p.op("scalar", lambda q, dst=dst, k=k: q.copy(out=dst[:], in_=pb4[:, k * 16:(k + 1) * 16]),
             s_be)
    p.wait("scalar", s_be, s_be.n)
    p.op("scalar", lambda q: q.activation(out=st["t1"][:], in_=st["ldT"][:], func=AF.Abs), s_bs)
    p.wait("scalar", s_bs, s_bs.n)
    p.op("scalar", lambda q: q.activation(out=st["t2"][:], in_=st["t1"][:], func=AF.Exp, scale=-1.0),
         s_bs)
    p.wait("scalar", s_bs, s_bs.n)
    p.op("scalar", lambda q: q.activation(out=st["t3"][:], in_=st["t2"][:], func=AF.Ln, bias=1.0),
         s_bs)
    p.wait("vector", s_bs, s_bs.n)
    p.op("vector", lambda q: q.tensor_scalar(out=st["t4"][:], in0=st["ldT"][:],
                                             scalar1=0.0, scalar2=None, op0=OP.max), s_bv)
    p.wait("vector", s_bv, s_bv.n)
    p.op("vector", lambda q: q.tensor_tensor(out=st["sp"][:], in0=st["t3"][:], in1=st["t4"][:],
                                             op=OP.add), s_bv)
    p.wait("scalar", s_bv, s_bv.n)
    p.op("scalar", lambda q: q.activation(out=st["a_diag"][:], in_=st["sp"][:], func=AF.Exp,
                                          scale=-1.0), s_bs)
    adiag_ready = s_bs.n
    p.wait("vector", s_m, 32)
    for dt in range(ND):
        p.op("vector", lambda q, dt=dt: q.tensor_tensor(out=st["abdump"][:],
                                                        in0=st["A_sb"][:, dt * 64:(dt + 1) * 64],
                                                        in1=st["B_sb"][:, dt * 64:(dt + 1) * 64],
                                                        op=OP.mult), s_bv)
        p.wait("vector", s_bv, s_bv.n)
        p.op("vector", lambda q, dt=dt: q.reduce_sum(out=st["AB"][:, dt:dt + 1],
                                                     in_=st["abdump"][:],
                                                     axis=mybir.AxisListType.X), s_bv)
    p.wait("vector", s_bv, s_bv.n)

    s_w = p.sem("p1w")
    s_t = p.sem("p1t")
    s_s = p.sem("p1s")
    s_v = p.sem("p1v")
    s_sc = p.sem("p1sc")
    cwr, dwr = st["cw_ring"], st["dw_ring"]
    pb = st["pb"]

    def load_wblk(dt):
        b = dt % 2
        if dt >= 2:
            p.wait("gpsimd", s_t, (dt - 2) * 4 + 4)
        p.dma("gpsimd", cwr[b][:],
              dr["C_wT"].rearrange("(kt p) o -> p kt o", p=128)[:, :, dt * 128:(dt + 1) * 128], s_w)
        p.dma("gpsimd", dwr[b][:],
              dr["D_wT"].rearrange("(kt p) o -> p kt o", p=128)[:, :, dt * 128:(dt + 1) * 128], s_w)

    load_wblk(0)
    load_wblk(1)
    for dt in range(ND):
        b = dt % 2
        if dt < ND - 2:
            load_wblk(dt + 2)
        p.wait("tensor", s_w, 32 * (dt + 1))
        if dt == 0:
            p.wait("tensor", s_ln, ln_done)
        for tc in range(4):
            g = dt * 4 + tc
            pc = pb[g % 2]
            pd = pb[2 + g % 2]
            if g >= 2:
                p.wait("tensor", s_v, g - 1)
                p.wait("tensor", s_s, g - 1)
            for kt in range(16):
                p.op("tensor",
                     lambda q, b=b, kt=kt, tc=tc, pc=pc:
                     q.matmul(pc[:], cwr[b][:, kt * 128:(kt + 1) * 128],
                              st["xn1T"][kt][:, tc * 512:(tc + 1) * 512],
                              start=(kt == 0), stop=(kt == 15)))
            for kt in range(16):
                p.op("tensor",
                     lambda q, b=b, kt=kt, tc=tc, pd=pd:
                     q.matmul(pd[:], dwr[b][:, kt * 128:(kt + 1) * 128],
                              st["xn1T"][kt][:, tc * 512:(tc + 1) * 512],
                              start=(kt == 0), stop=(kt == 15)),
                     s_t if kt == 15 else None)
            p.wait("scalar", s_t, s_t.n)
            if g >= 2:
                p.wait("scalar", s_v, g - 1)
            sigbuf = st["sig"][g % 2]
            p.op("scalar",
                 lambda q, dt=dt, pd=pd, sigbuf=sigbuf:
                 q.activation(out=sigbuf[:], in_=pd[:], func=AF.Sigmoid,
                              bias=st["dbT"][:, dt:dt + 1]), s_s)
            p.wait("vector", s_s, s_s.n)
            xpb = st["xp"]
            p.op("vector",
                 lambda q, dt=dt, tc=tc, pc=pc, sigbuf=sigbuf, xpb=xpb:
                 q.scalar_tensor_tensor(out=xpb[:, tc * 512:(tc + 1) * 512], in0=pc[:],
                                        scalar=st["cbT"][:, dt:dt + 1], in1=sigbuf[:],
                                        op0=OP.add, op1=OP.mult), s_v)
        if dt == 0:
            p.wait("vector", s_bs, adiag_ready)
        xpb = st["xp"]
        p.op("vector",
             lambda q, dt=dt, xpb=xpb:
             q.tensor_tensor_scan(out=st["gs"][:],
                                  data0=st["a_diag"][:, dt:dt + 1].broadcast_to([128, T]),
                                  data1=xpb[:], initial=0.0, op0=OP.mult, op1=OP.add), s_sc)
        p.wait("vector", s_sc, s_sc.n)
        p.op("vector",
             lambda q, dt=dt:
             q.tensor_scalar(out=st["attn_outT"][dt][:], in0=st["gs"][:],
                             scalar1=st["AB"][:, dt:dt + 1], scalar2=None, op0=OP.mult), s_sc)
    scanall = s_sc.n

    s_x2 = p.sem("p1x2")
    s_st = p.sem("p1st")
    xt = st["xtile"]
    outst = st["outstage"]
    mill = TransposeMill(p, st["pb"][4:6], "p1d", BF16, "vector")
    p.wait("tensor", s_sc, scanall)
    p.wait("sync", s_sc, scanall)
    for i in range(NT):
        b = i % 2
        if i >= 2:
            p.wait("sync", mill.s_done, (i - 2) * 16 + 16)
        p.dma("sync", xt[b][:], dr["x"][i * 128:(i + 1) * 128, :], s_x2)
        p.wait("vector", s_x2, 16 * (i + 1))
        if i >= 2:
            p.wait("vector", s_st, 16 * (i - 1))

        def src_fn(dt, i=i):
            return st["attn_outT"][dt][:, i * 128:(i + 1) * 128]

        def evac_fn(dt, pv, b=b):
            p.op("vector",
                 lambda q, dt=dt, pv=pv, b=b:
                 q.tensor_tensor(out=outst[b][:, dt * 128:(dt + 1) * 128], in0=pv,
                                 in1=xt[b][:, dt * 128:(dt + 1) * 128], op=OP.add),
                 mill.s_done)

        mill.go(src_fn, evac_fn, ND, st["ident_bf"][:])
        p.wait("sync", mill.s_done, i * 16 + 16)
        p.dma("sync", dr["xc1"][i * 128:(i + 1) * 128, :], outst[b][:], s_st)
    p.wait("sync", s_st, s_st.n)


def alloc_common(nc, stack, st):
    sb = lambda name, shape, dt: stack.enter_context(nc.sbuf_tensor(name, shape, dt))
    st["pb"] = [stack.enter_context(nc.psum_tensor("pb%d" % k, [128, 512], F32))
                for k in range(8)]
    st["ident_bf"] = sb("ident_bf", [128, 128], BF16)
    st["ident_f32"] = sb("ident_f32", [128, 128], F32)
    st["xtile"] = [sb("xtile%d" % k, [128, D], F32) for k in range(2)]
    st["xnbf"] = [sb("xnbf%d" % k, [128, D], BF16) for k in range(2)]
    st["stats"] = sb("stats", [128, 24], F32)
    st["mv"] = sb("mv", [128, 2], F32)
    st["negmean"] = sb("negmean", [128, 1], F32)
    st["veps"] = sb("veps", [128, 1], F32)
    st["rstd"] = sb("rstd", [128, 1], F32)


def alloc_p1(nc, stack, st):
    sb = lambda name, shape, dt: stack.enter_context(nc.sbuf_tensor(name, shape, dt))
    st["outstage"] = [sb("outstage%d" % k, [128, D], F32) for k in range(2)]
    st["xn1T"] = [sb("xn1T_%d" % k, [128, T], BF16) for k in range(ND)]
    st["attn_outT"] = [sb("aoT_%d" % k, [128, T], BF16) for k in range(ND)]
    st["A_sb"] = sb("A_sb", [128, ND * 64], F32)
    st["B_sb"] = sb("B_sb", [128, ND * 64], F32)
    st["abdump"] = sb("abdump", [128, 64], F32)
    st["AB"] = sb("AB", [128, ND], F32)
    for nm in ("ld16", "cb16", "db16"):
        st[nm] = sb(nm, [16, 128], F32)
    for nm in ("ldT", "cbT", "dbT", "t1", "t2", "t3", "t4", "sp", "a_diag"):
        st[nm] = sb(nm, [128, 16], F32)
    st["cw_ring"] = [sb("cwr%d" % k, [128, ND * 128], BF16) for k in range(2)]
    st["dw_ring"] = [sb("dwr%d" % k, [128, ND * 128], BF16) for k in range(2)]
    st["sig"] = [sb("sig%d" % k, [128, 512], F32) for k in range(2)]
    st["xp"] = sb("xp", [128, T], F32)
    st["gs"] = st["outstage"][0]


def build_p1(standalone=True, dbg=False):
    nc = bass.Bass("TRN2", target_bir_lowering=False)
    dr = {}
    ext_in = lambda n, s, d: nc.dram_tensor(n, s, d, kind="ExternalInput").ap()
    dr["x"] = ext_in("x", [T, D], F32)
    dr["C_wT"] = ext_in("C_wT", [D, D], F32)
    dr["D_wT"] = ext_in("D_wT", [D, D], F32)
    dr["C_b"] = ext_in("C_b", [D], F32)
    dr["D_b"] = ext_in("D_b", [D], F32)
    dr["A"] = ext_in("A", [D, 64], F32)
    dr["B"] = ext_in("B", [D, 64], F32)
    dr["log_delta"] = ext_in("log_delta", [D], F32)
    dr["ident_bf_h"] = ext_in("ident_bf_h", [128, 128], BF16)
    dr["ident_f32_h"] = ext_in("ident_f32_h", [128, 128], F32)
    dr["xc1"] = nc.dram_tensor("xc1", [T, D], F32, kind="ExternalOutput").ap()
    if dbg:
        dr["dbg_xn1T"] = nc.dram_tensor("dbg_xn1T", [D, T], BF16, kind="ExternalOutput").ap()
        dr["dbg_aoT"] = nc.dram_tensor("dbg_aoT", [D, T], BF16, kind="ExternalOutput").ap()
        dr["dbg_xp15"] = nc.dram_tensor("dbg_xp15", [128, T], F32, kind="ExternalOutput").ap()
        dr["dbg_adiag"] = nc.dram_tensor("dbg_adiag", [128, 16], F32, kind="ExternalOutput").ap()
        dr["dbg_AB"] = nc.dram_tensor("dbg_AB", [128, 16], F32, kind="ExternalOutput").ap()
    with contextlib.ExitStack() as stack:
        st = {}
        alloc_common(nc, stack, st)
        alloc_p1(nc, stack, st)
        st["dbg"] = dbg
        p = Prog(nc)
        s_c = p.sem("const")
        p.dma("sync", st["ident_bf"][:], dr["ident_bf_h"], s_c)
        p.dma("sync", st["ident_f32"][:], dr["ident_f32_h"], s_c)
        p.wait("tensor", s_c, 32)
        emit_p1(p, st, dr)
        if dbg:
            s_d = p.sem("dbgst")
            for dt in range(ND):
                p.dma("sync", dr["dbg_xn1T"][dt * 128:(dt + 1) * 128, :], st["xn1T"][dt][:], s_d)
                p.dma("sync", dr["dbg_aoT"][dt * 128:(dt + 1) * 128, :], st["attn_outT"][dt][:], s_d)
            p.dma("sync", dr["dbg_xp15"], st["xp"][:], s_d)
            p.dma("sync", dr["dbg_adiag"], st["a_diag"][:], s_d)
            p.dma("sync", dr["dbg_AB"], st["AB"][:], s_d)
            p.wait("sync", s_d, s_d.n)
        p.finalize(stack)
    return nc


_IDENT_BF = np.eye(128, dtype=bfdt)
_IDENT_F32 = np.eye(128, dtype=np.float32)


def _p1_inputs(inp, core):
    return {
        "x": np.ascontiguousarray(np.asarray(inp["L0_attn_x"])[core]),
        "C_wT": np.ascontiguousarray(np.asarray(inp["L0_attn_C_w"]).T),
        "D_wT": np.ascontiguousarray(np.asarray(inp["L0_attn_D_w"]).T),
        "C_b": np.asarray(inp["L0_attn_C_b"]),
        "D_b": np.asarray(inp["L0_attn_D_b"]),
        "A": np.asarray(inp["L0_attn_A"]),
        "B": np.asarray(inp["L0_attn_B"]),
        "log_delta": np.asarray(inp["L0_attn_log_delta"]),
        "ident_bf_h": _IDENT_BF,
        "ident_f32_h": _IDENT_F32,
    }


def kernel(**inputs):
    cores = list(range(8))
    nc1 = build_p1()
    res1 = run_bass_kernel_spmd(nc1, [_p1_inputs(inputs, c) for c in cores], core_ids=cores)
    xc1 = [np.asarray(res1.results[c]["xc1"]) for c in cores]
    nc2 = build_p2()
    b2 = _p2_prep(inputs)
    res2 = run_bass_kernel_spmd(nc2, [dict(b2, xc1=xc1[c]) for c in cores], core_ids=cores)
    xc2 = [np.asarray(res2.results[c]["xc2"]) for c in cores]
    nc3 = build_p3()
    b3 = _p3_prep(inputs)
    res3 = run_bass_kernel_spmd(nc3, [dict(b3, xc2=xc2[c]) for c in cores], core_ids=cores)
    xc3 = [np.asarray(res3.results[c]["xc3"]) for c in cores]
    nc4 = build_p4()
    b4 = _p4_prep(inputs)
    res4 = run_bass_kernel_spmd(nc4, [dict(b4, xc3=xc3[c]) for c in cores], core_ids=cores)
    return np.stack([np.asarray(res4.results[c]["out"]) for c in cores], axis=0)


def emit_p2(p, st, dr):
    NQ, QT = 4, 512
    NFB = 32
    s_c = p.sem("p2c")
    s_ct = p.sem("p2ct")
    s_ce = p.sem("p2ce")
    p.dma("sync", st["wrT_raw"][:], dr["WrT"], s_c)
    p.dma("sync", st["b1_64"][:], dr["b1"].rearrange("(a b) -> a b", b=128), s_c)
    p.dma("sync", st["br_bc"][:], dr["br"].unsqueeze(0).unsqueeze(2).rearrange("a b c -> a c b").broadcast_to([1, 128, 8]), s_c)
    p.dma("gpsimd", st["a1p_sb"][:],
          dr["A1p"].rearrange("(kt p) r -> p kt r", p=128), s_c)
    p.dma("gpsimd", st["B2p_sb"][0][:], dr["B2p"][0:128, :], s_c)
    p.dma("gpsimd", st["B2p_sb"][1][:], dr["B2p"][128:256, :], s_c)
    p.wait("tensor", s_c, s_c.n)
    pbt = st["pb"][6]
    for kt in range(16):
        p.op("tensor",
             lambda q, kt=kt: q.transpose(pbt[:, kt * 8:(kt + 1) * 8],
                                          st["wrT_raw"][:, kt * 128:(kt + 1) * 128],
                                          st["ident_f32"][0:8, 0:8]),
             s_ct)
    p.op("tensor", lambda q: q.transpose(st["pb"][7][:, 0:64], st["b1_64"][:],
                                         st["ident_f32"][0:64, 0:64]), s_ct)
    p.wait("scalar", s_ct, s_ct.n)
    for kt in range(16):
        p.op("scalar", lambda q, kt=kt: q.copy(out=st["wr_sb"][:, kt * 8:(kt + 1) * 8],
                                               in_=pbt[:, kt * 8:(kt + 1) * 8]), s_ce)
    p.op("scalar", lambda q: q.copy(out=st["b1f"][:], in_=st["pb"][7][:, 0:64]), s_ce)

    s_st2 = p.sem("p2st")
    p.wait("tensor", s_ce, s_ce.n)
    lnb = make_ln_bundle(p, st, "p2ln")
    s_rt = p.sem("p2rt")
    s_rv = p.sem("p2rv")
    s_rs = p.sem("p2rs")
    s_cb = p.sem("p2cb")
    s_w = p.sem("p2w")
    s_bt = p.sem("p2bt")
    s_bs = p.sem("p2bs")
    s_et = p.sem("p2et")
    s_es = p.sem("p2es")
    s_ev = p.sem("p2ev")
    s_mx = p.sem("p2mx")
    s_wt = p.sem("p2wt")
    s_wv = p.sem("p2wv")
    s_t2d = p.sem("p2t2")
    s_em = p.sem("p2em")
    s_d1 = p.sem("p2d1")
    s_d2 = p.sem("p2d2")
    s_lt = p.sem("p2lt")
    s_lv = p.sem("p2lv")
    s_b2 = p.sem("p2b2")
    for qt in range(NQ):
        def acc_init(p_, i, b, xt):
            p_.op("vector", lambda q, i=i, b=b: q.tensor_copy(out=st["acc"][i][:],
                                                              in_=xt[b][:]))

        s_ln = emit_ln_transpose(p, st, dr["xc1"], st["xn2T"], "p2a%d" % qt,
                                 tiles=[(qt * 4 + i, i) for i in range(4)],
                                 post_tile=acc_init, bundle=lnb)
        p.wait("tensor", s_ln, s_ln.n)

        pbr = st["pb"][6]
        pbrv = pbr[:].bitcast(BF16)
        pbt1 = st["pb"][7]
        for i in range(4):
            p.wait("tensor", s_rs, s_rs.n)
            for kt in range(16):
                p.op("tensor",
                     lambda q, kt=kt, i=i: q.matmul(pbr[:, 0:8],
                                                    st["xn2T"][kt][:, i * 128:(i + 1) * 128],
                                                    st["wr_sb"][:, kt * 8:(kt + 1) * 8],
                                                    start=(kt == 0), stop=(kt == 15)),
                     s_rt if kt == 15 else None)
            p.wait("vector", s_rt, s_rt.n)
            r = st["r8"]

            def V(fn, mark=False):
                p.op("vector", fn, s_rv if mark else None)
                if mark:
                    p.wait("vector", s_rv, s_rv.n)

            V(lambda q: q.tensor_tensor(out=r["logits"][:], in0=pbr[:, 0:8],
                                        in1=st["br_bc"][:], op=OP.add), True)
            V(lambda q: q.tensor_reduce(out=r["m1"][:], in_=r["logits"][:],
                                        axis=mybir.AxisListType.X, op=OP.max, negate=True), True)
            p.wait("scalar", s_rv, s_rv.n)
            p.op("scalar", lambda q: q.activation(out=r["exp"][:], in_=r["logits"][:],
                                                  func=AF.Exp, bias=r["m1"][:]), s_rs)
            p.wait("vector", s_rs, s_rs.n)
            V(lambda q: q.reduce_sum(out=r["s"][:], in_=r["exp"][:],
                                     axis=mybir.AxisListType.X), True)
            V(lambda q: q.reciprocal(out=r["s"][:], in_=r["s"][:]), True)
            V(lambda q: q.tensor_scalar(out=r["probs"][:], in0=r["exp"][:], scalar1=r["s"][:],
                                        scalar2=None, op0=OP.mult), True)
            V(lambda q: q.tensor_reduce(out=r["m1"][:], in_=r["probs"][:],
                                        axis=mybir.AxisListType.X, op=OP.max), True)
            V(lambda q: q.tensor_scalar(out=r["mask"][:], in0=r["probs"][:], scalar1=r["m1"][:],
                                        scalar2=None, op0=OP.is_ge), True)
            V(lambda q: q.tensor_tensor(out=r["pm"][:], in0=r["probs"][:], in1=r["mask"][:],
                                        op=OP.mult), True)
            V(lambda q: q.tensor_tensor(out=r["pm"][:], in0=r["probs"][:], in1=r["pm"][:],
                                        op=OP.subtract), True)
            V(lambda q: q.tensor_reduce(out=r["m2"][:], in_=r["pm"][:],
                                        axis=mybir.AxisListType.X, op=OP.max), True)
            V(lambda q: q.tensor_scalar(out=r["mask"][:], in0=r["probs"][:], scalar1=r["m2"][:],
                                        scalar2=None, op0=OP.is_ge), True)
            V(lambda q: q.tensor_tensor(out=r["comb"][:], in0=r["probs"][:], in1=r["mask"][:],
                                        op=OP.mult), True)
            V(lambda q, i=i: q.reduce_sum(out=st["cbtot"][:, i:i + 1], in_=r["comb"][:],
                                          axis=mybir.AxisListType.X), True)
            p.wait("tensor", s_rv, s_rv.n)
            p.op("tensor", lambda q: q.transpose(pbr[0:8, 128:256], r["comb"][:],
                                                 st["ident_f32"][:]), s_rt)
            p.wait("scalar", s_rt, s_rt.n)
            p.op("scalar", lambda q, i=i: q.copy(out=st["comb8"][:, i * 128:(i + 1) * 128],
                                                 in_=pbr[0:8, 128:256]), s_rs)
            p.wait("tensor", s_rs, s_rs.n)
            for kt in range(16):
                p.op("tensor",
                     lambda q, kt=kt, i=i: q.matmul(pbt1[:, 0:256],
                                                    st["xn2T"][kt][:, i * 128:(i + 1) * 128],
                                                    st["a1p_sb"][:, kt * 256:(kt + 1) * 256],
                                                    start=(kt == 0), stop=(kt == 15)),
                     s_rt if kt == 15 else None)
            p.wait("scalar", s_rt, s_rt.n)
            p.op("scalar", lambda q: q.mul(st["t1sb"][:], pbt1[:, 0:256], 2.0), s_rs)
            p.wait("tensor", s_rs, s_rs.n)
            for e in range(E):
                p.op("tensor",
                     lambda q, e=e: q.transpose(pbrv[0:32, (e % 4) * 128:(e % 4 + 1) * 128],
                                                st["t1sb"][:, e * 32:(e + 1) * 32],
                                                st["ident_bf"][:]),
                     s_rt if e % 4 == 3 else None)
                if e % 4 == 3:
                    p.wait("scalar", s_rt, s_rt.n)
                    for ee in range(e - 3, e + 1):
                        p.op("scalar",
                             lambda q, ee=ee, i=i:
                             q.copy(out=st["t1T"][ee][:, i * 128:(i + 1) * 128],
                                    in_=pbrv[0:32, (ee % 4) * 128:(ee % 4 + 1) * 128]),
                             s_rs)
                    if e == 3:
                        p.wait("tensor", s_rs, s_rs.n)
        p.wait("sync", s_rs, s_rs.n)
        bcb = s_cb.n
        p.dma("sync", dr["combT"][:, qt * QT:(qt + 1) * QT], st["comb8"][:], s_cb)
        p.wait("gpsimd", s_cb, bcb + 16)
        for e in range(E):
            p.dma("gpsimd", st["comb_bc"][e][:],
                  dr["combT"][e:e + 1, qt * QT:(qt + 1) * QT].unsqueeze(1)
                  .broadcast_to([1, 128, QT]), s_cb)
        p.wait("sync", s_cb, bcb + 16)
        p.dma("sync", st["cbstack"][0][:],
              dr["combT"][0:4, qt * QT:(qt + 1) * QT].unsqueeze(1).broadcast_to([4, 32, QT]),
              s_cb)
        p.dma("sync", st["cbstack"][1][:],
              dr["combT"][4:8, qt * QT:(qt + 1) * QT].unsqueeze(1).broadcast_to([4, 32, QT]),
              s_cb)
        cb_ready = s_cb.n

        bw, bbt, bbs = s_w.n, s_bt.n, s_bs.n
        bet, bes, bev = s_et.n, s_es.n, s_ev.n
        bmx, bwt, bwv = s_mx.n, s_wt.n, s_wv.n

        def load_fb(fb):
            bb = fb % 2
            if fb >= 2:
                p.wait("gpsimd", s_bt, bbt + (fb - 1) * 2 + 1)
                p.wait("gpsimd", s_et, bet + (fb - 2) * 16 + 16)
            p.dma("gpsimd", st["w1blk"][bb][:],
                  dr["W1"].rearrange("(kt p) f -> p kt f", p=128)[:, :, fb * 256:(fb + 1) * 256],
                  s_w)
            p.dma("gpsimd", st["b1pblk"][bb][:],
                  dr["B1p"][:, :, fb * 256:(fb + 1) * 256].rearrange("e r f -> r e f"), s_w)
            p.dma("gpsimd", st["a2blk"][bb][:],
                  dr["A2p2"][fb * 256:(fb + 1) * 256, :].rearrange("(ft p) c -> p ft c", p=128),
                  s_w)
            if fb >= 3:
                p.wait("gpsimd", s_wt, bwt + ((fb - 3) // 2) * 16 + 16)
            p.dma("gpsimd", st["w2blk"][fb % 3][:],
                  dr["W2"][fb * 256:(fb + 1) * 256, :].rearrange("(ft p) d -> p ft d", p=128),
                  s_w)

        load_fb(0)
        load_fb(1)
        p.wait("vector", s_cb, cb_ready)
        p.wait("tensor", s_rs, s_rs.n)
        for fb in range(NFB):
            bb = fb % 2
            if fb < NFB - 2:
                load_fb(fb + 2)
            p.wait("tensor", s_w, bw + 64 * (fb + 1))
            for ft in range(2):
                ftg = fb * 2 + ft
                pbase = st["pb"][ftg % 2]
                if ftg >= 2:
                    p.wait("tensor", s_bs, bbs + ftg - 1)
                for kt in range(16):
                    p.op("tensor",
                         lambda q, kt=kt, ft=ft, bb=bb, pbase=pbase:
                         q.matmul(pbase[:],
                                  st["w1blk"][bb][:, kt * 256 + ft * 128:kt * 256 + ft * 128 + 128],
                                  st["xn2T"][kt][:], start=(kt == 0), stop=(kt == 15)),
                         s_bt if kt == 15 else None)
                p.wait("scalar", s_bt, s_bt.n)
                if ftg >= 2:
                    p.wait("scalar", s_et, bet + (ftg - 1) * 8)
                p.op("scalar", lambda q, ftg=ftg, pbase=pbase:
                     q.copy(out=st["base_bf"][ftg % 2][:], in_=pbase[:]), s_bs)
                for e in range(E):
                    ge = ftg * 8 + e
                    pe = st["pb"][2 + e % 2]
                    if ge >= 2:
                        p.wait("tensor", s_es, bes + ge - 1)
                    if e == 0:
                        p.wait("tensor", s_bs, s_bs.n)
                    p.op("tensor",
                         lambda q, e=e, ft=ft, bb=bb, pe=pe:
                         q.matmul(pe[:],
                                  st["b1pblk"][bb][0:32, e * 256 + ft * 128:e * 256 + ft * 128 + 128],
                                  st["t1T"][e][:], start=True, stop=False))
                    p.op("tensor",
                         lambda q, pe=pe, ftg=ftg:
                         q.matmul(pe[:], st["ident_bf"][:], st["base_bf"][ftg % 2][:],
                                  start=False, stop=True), s_et)
                    p.wait("scalar", s_et, s_et.n)
                    if ge >= 2:
                        p.wait("scalar", s_ev, bev + ge - 1)
                    p.op("scalar",
                         lambda q, pe=pe, e=e, ftg=ftg:
                         q.activation(out=st["aeT"][e % 2][:], in_=pe[:],
                                      func=AF.Gelu_apprx_tanh,
                                      bias=st["b1f"][:, ftg:ftg + 1]), s_es)
                    p.wait("vector", s_es, s_es.n)
                    p.op("vector",
                         lambda q, e=e: q.tensor_tensor(out=st["ca"][e % 2][:],
                                                        in0=st["aeT"][e % 2][:],
                                                        in1=st["comb_bc"][e][:], op=OP.mult),
                         s_ev)
                    if e == 0:
                        if ftg >= 2:
                            p.wait("vector", s_mx, bmx + ftg - 1)
                        p.op("vector",
                             lambda q, ftg=ftg: q.tensor_copy(out=st["mixT"][ftg % 2][:],
                                                              in_=st["ca"][0][:]), s_em)
                    else:
                        p.op("vector",
                             lambda q, e=e, ftg=ftg:
                             q.tensor_tensor(out=st["mixT"][ftg % 2][:],
                                             in0=st["mixT"][ftg % 2][:],
                                             in1=st["ca"][e % 2][:], op=OP.add), s_em)
                    p.wait("tensor", s_es, s_es.n)
                    p.op("tensor",
                         lambda q, e=e, ft=ft, bb=bb, ftg=ftg:
                         q.matmul(st["pb"][4 + e // 4][32 * (e % 4):32 * (e % 4) + 32, :],
                                  st["a2blk"][bb][:, ft * 256 + e * 32:ft * 256 + e * 32 + 32],
                                  st["aeT"][e % 2][:], start=(ftg == 0),
                                  stop=(ftg == NFB * 2 - 1),
                                  tile_position=(0, 32 * (e % 4))),
                         s_t2d if (ftg == NFB * 2 - 1 and e == E - 1) else None)
                p.wait("scalar", s_em, s_em.n)
                if ftg >= 8:
                    p.wait("scalar", s_wt, bwt + 16 * (ftg // 4 - 1))
                p.op("scalar", lambda q, ftg=ftg: q.copy(out=st["mixT_bf"][ftg % 8][:],
                                                         in_=st["mixT"][ftg % 2][:]), s_mx)
            if fb % 2 == 1:
                blk = fb // 2
                p.wait("tensor", s_mx, s_mx.n)
                for tt in range(4):
                    for db in range(4):
                        idx = tt * 4 + db
                        pw = st["pb"][6 + idx % 2]
                        if blk * 16 + idx >= 2:
                            p.wait("tensor", s_wv, bwv + blk * 16 + idx - 1)
                        for j in range(4):
                            g = 4 * blk + j
                            p.op("tensor",
                                 lambda q, g=g, j=j, tt=tt, db=db, pw=pw, blk=blk:
                                 q.matmul(pw[:], st["mixT_bf"][g % 8][:, tt * 128:(tt + 1) * 128],
                                          st["w2blk"][(2 * blk + j // 2) % 3]
                                          [:, (j % 2) * 2048 + db * 512:(j % 2) * 2048 + db * 512 + 512],
                                          start=(j == 0), stop=(j == 3)),
                                 s_wt if j == 3 else None)
                        p.wait("vector", s_wt, s_wt.n)
                        p.op("vector",
                             lambda q, tt=tt, db=db, pw=pw:
                             q.tensor_tensor(out=st["acc"][tt][:, db * 512:(db + 1) * 512],
                                             in0=pw[:],
                                             in1=st["acc"][tt][:, db * 512:(db + 1) * 512],
                                             op=OP.add), s_wv)

        blv = s_lv.n
        bb2 = s_b2.n
        p.wait("vector", s_t2d, s_t2d.n)
        for j in range(2):
            p.op("vector",
                 lambda q, j=j: q.scalar_tensor_tensor(out=st["t2acc"][j][:],
                                                       in0=st["pb"][4 + j][:],
                                                       scalar=2.0, in1=st["cbstack"][j][:],
                                                       op0=OP.mult, op1=OP.mult), s_d1)
        p.wait("scalar", s_d1, s_d1.n)
        for j in range(2):
            p.op("scalar", lambda q, j=j: q.copy(out=st["t2acc_bf"][j][:],
                                                 in_=st["t2acc"][j][:]), s_d2)
        p.wait("tensor", s_d2, s_d2.n)
        for tt in range(4):
            for db in range(4):
                gidx = tt * 4 + db
                pl = st["pb"][6 + db % 2]
                if gidx >= 2:
                    p.wait("sync", s_lv, blv + gidx - 1)
                p.dma("sync", st["b2_bc"][db % 2][:],
                      dr["b2"][db * 512:(db + 1) * 512].unsqueeze(0).unsqueeze(0)
                      .broadcast_to([1, 128, 512]), s_b2)
                if gidx >= 2:
                    p.wait("tensor", s_lv, blv + gidx - 1)
                p.op("tensor",
                     lambda q, tt=tt, db=db, pl=pl:
                     q.matmul(pl[:], st["t2acc_bf"][0][:, tt * 128:(tt + 1) * 128],
                              st["B2p_sb"][0][:, db * 512:(db + 1) * 512], start=True,
                              stop=False))
                p.op("tensor",
                     lambda q, tt=tt, db=db, pl=pl:
                     q.matmul(pl[:], st["t2acc_bf"][1][:, tt * 128:(tt + 1) * 128],
                              st["B2p_sb"][1][:, db * 512:(db + 1) * 512], start=False,
                              stop=True), s_lt)
                p.wait("vector", s_lt, s_lt.n)
                p.wait("vector", s_b2, bb2 + 16 * (gidx + 1))
                p.op("vector",
                     lambda q, tt=tt, db=db, pl=pl:
                     q.scalar_tensor_tensor(out=st["ltmp"][:],
                                            in0=st["b2_bc"][db % 2][:],
                                            scalar=st["cbtot"][:, tt:tt + 1], in1=pl[:],
                                            op0=OP.mult, op1=OP.add))
                p.op("vector",
                     lambda q, tt=tt, db=db:
                     q.tensor_tensor(out=st["acc"][tt][:, db * 512:(db + 1) * 512],
                                     in0=st["ltmp"][:],
                                     in1=st["acc"][tt][:, db * 512:(db + 1) * 512], op=OP.add),
                     s_lv)
            p.wait("sync", s_lv, s_lv.n)
            p.dma("sync", dr["xc2"][(qt * 4 + tt) * 128:(qt * 4 + tt + 1) * 128, :],
                  st["acc"][tt][:], s_st2)
        for eng in ENGINES:
            p.wait(eng, s_st2, s_st2.n)
    return s_st2


def alloc_p2(nc, stack, st):
    sb = lambda name, shape, dt: stack.enter_context(nc.sbuf_tensor(name, shape, dt))
    st["xn2T"] = [sb("xn2T_%d" % k, [128, 512], BF16) for k in range(16)]
    st["acc"] = [sb("acc%d" % k, [128, D], F32) for k in range(4)]
    st["w1blk"] = [sb("w1blk%d" % k, [128, 16 * 256], BF16) for k in range(2)]
    st["b1pblk"] = [sb("b1pblk%d" % k, [32, 8 * 256], BF16) for k in range(2)]
    st["a2blk"] = [sb("a2blk%d" % k, [128, 2 * 256], BF16) for k in range(2)]
    st["w2blk"] = [sb("w2blk%d" % k, [128, 2 * 2048], BF16) for k in range(3)]
    st["comb_bc"] = [sb("combbc%d" % k, [128, 512], BF16) for k in range(E)]
    st["cbstack"] = [sb("cbstack%d" % k, [128, 512], F32) for k in range(2)]
    st["mixT"] = [sb("mixT%d" % k, [128, 512], F32) for k in range(2)]
    st["mixT_bf"] = [sb("mixTb%d" % k, [128, 512], BF16) for k in range(8)]
    st["t2acc"] = [sb("t2acc%d" % k, [128, 512], F32) for k in range(2)]
    st["t2acc_bf"] = [sb("t2accb%d" % k, [128, 512], BF16) for k in range(2)]
    st["B2p_sb"] = [sb("B2p_sb%d" % k, [128, D], BF16) for k in range(2)]
    st["t1T"] = [sb("t1T%d" % k, [32, 512], BF16) for k in range(E)]
    st["t1sb"] = sb("t1sb", [128, 256], BF16)
    st["aeT"] = [sb("aeT%d" % k, [128, 512], BF16) for k in range(2)]
    st["ca"] = [sb("ca%d" % k, [128, 512], BF16) for k in range(2)]
    st["base_bf"] = [sb("basebf%d" % k, [128, 512], BF16) for k in range(2)]
    st["ltmp"] = sb("ltmp", [128, 512], F32)
    st["wrT_raw"] = sb("wrT_raw", [8, 2048], F32)
    st["wr_sb"] = sb("wr_sb", [128, 128], BF16)
    st["a1p_sb"] = sb("a1p_sb", [128, 16 * 256], BF16)
    st["b1_64"] = sb("b1_64", [64, 128], F32)
    st["b1f"] = sb("b1f", [128, 64], F32)
    st["br_bc"] = sb("br_bc", [128, 8], F32)
    st["b2_bc"] = [sb("b2_bc%d" % k, [128, 512], F32) for k in range(2)]
    st["cbtot"] = sb("cbtot", [128, 4], F32)
    st["comb8"] = sb("comb8", [8, 512], F32)
    r = {}
    for nm in ("logits", "exp", "probs", "pm", "mask", "comb"):
        r[nm] = sb("r8_" + nm, [128, 8], F32)
    for nm in ("m1", "m2", "s"):
        r[nm] = sb("r8_" + nm, [128, 1], F32)
    st["r8"] = r


def build_p2(standalone=True):
    nc = bass.Bass("TRN2", target_bir_lowering=False)
    dr = {}
    ext_in = lambda n, s, d: nc.dram_tensor(n, s, d, kind="ExternalInput").ap()
    dr["xc1"] = ext_in("xc1", [T, D], F32)
    dr["WrT"] = ext_in("WrT", [8, D], F32)
    dr["br"] = ext_in("br", [8], F32)
    dr["W1"] = ext_in("W1", [D, F], F32)
    dr["b1"] = ext_in("b1", [F], F32)
    dr["W2"] = ext_in("W2", [F, D], F32)
    dr["b2"] = ext_in("b2", [D], F32)
    dr["A1p"] = ext_in("A1p", [D, 256], F32)
    dr["B1p"] = ext_in("B1p", [E, 32, F], F32)
    dr["A2p2"] = ext_in("A2p2", [F, 256], F32)
    dr["B2p"] = ext_in("B2p", [256, D], F32)
    dr["ident_bf_h"] = ext_in("ident_bf_h", [128, 128], BF16)
    dr["ident_f32_h"] = ext_in("ident_f32_h", [128, 128], F32)
    dr["combT"] = nc.dram_tensor("combT", [8, T], F32, kind="Internal").ap()
    dr["xc2"] = nc.dram_tensor("xc2", [T, D], F32, kind="ExternalOutput").ap()
    with contextlib.ExitStack() as stack:
        st = {}
        alloc_common(nc, stack, st)
        alloc_p2(nc, stack, st)
        p = Prog(nc)
        s_c = p.sem("const")
        p.dma("sync", st["ident_bf"][:], dr["ident_bf_h"], s_c)
        p.dma("sync", st["ident_f32"][:], dr["ident_f32_h"], s_c)
        p.wait("tensor", s_c, 32)
        emit_p2(p, st, dr)
        p.finalize(stack)
    return nc


def _p2_prep(inp):
    A1 = np.asarray(inp["L0_ffn_A1"])
    B1 = np.asarray(inp["L0_ffn_B1"])
    A2 = np.asarray(inp["L0_ffn_A2"])
    B2 = np.asarray(inp["L0_ffn_B2"])
    A1p = np.zeros((D, E, 32), np.float32)
    A1p[:, :, :16] = A1.transpose(2, 0, 1)
    B1p = np.zeros((E, 32, F), np.float32)
    B1p[:, :16, :] = B1.transpose(0, 2, 1)
    A2p2 = np.zeros((F, E, 32), np.float32)
    A2p2[:, :, :16] = A2.transpose(2, 0, 1)
    B2p = np.zeros((E, 32, D), np.float32)
    B2p[:, :16, :] = B2.transpose(0, 2, 1)
    return {
        "WrT": np.ascontiguousarray(np.asarray(inp["L0_ffn_Wr"]).T),
        "br": np.asarray(inp["L0_ffn_br"]),
        "W1": np.asarray(inp["L0_ffn_W1"]),
        "b1": np.asarray(inp["L0_ffn_b1"]),
        "W2": np.asarray(inp["L0_ffn_W2"]),
        "b2": np.asarray(inp["L0_ffn_b2"]),
        "A1p": np.ascontiguousarray(A1p.reshape(D, 256)),
        "B1p": np.ascontiguousarray(B1p),
        "A2p2": np.ascontiguousarray(A2p2.reshape(F, 256)),
        "B2p": np.ascontiguousarray(B2p.reshape(256, D)),
        "ident_bf_h": _IDENT_BF,
        "ident_f32_h": _IDENT_F32,
    }


GAMMAS = [1.0 - 2.0 ** (-5.0 - h) for h in range(H)]


def emit_p3(p, st, dr):
    lnb = make_ln_bundle(p, st, "p3ln")
    s_ln = emit_ln_transpose(p, st, dr["xc2"], st["xn3T"], "p3a", bundle=lnb)
    p.wait("tensor", s_ln, s_ln.n)

    s_c = p.sem("p3c")
    p.dma("gpsimd", st["cos_sb"][:], dr["cos_h"].rearrange("(tt p) d -> p tt d", p=128), s_c)
    p.dma("gpsimd", st["sin_sb"][:], dr["sin_h"].rearrange("(tt p) d -> p tt d", p=128), s_c)
    p.dma("gpsimd", st["qs_sb"][:], dr["qs_h"].rearrange("(tt p) d -> p tt d", p=128), s_c)
    p.dma("gpsimd", st["ks_sb"][:], dr["ks_h"].rearrange("(tt p) d -> p tt d", p=128), s_c)
    for h in range(H):
        p.dma("gpsimd", st["mask"][h][:], dr["masks_h"][h], s_c)
    p.dma("sync", st["gpow"][:], dr["gpow_h"], s_c)
    p.dma("sync", st["gsc"][:], dr["gsc_h"], s_c)
    tabs_ready = s_c.n

    s_w = p.sem("p3w")
    s_mt = p.sem("p3mt")
    s_mv = p.sem("p3mv")
    s_rp = p.sem("p3rp")
    s_gall = p.sem("p3gall")
    wfree = {}
    vfree = {}
    pfree = {}

    for g in range(4):
        for mi, (wname, bname, dst) in enumerate(
                [("Wq", "bq", st["qg"]), ("Wk", "bk", st["kg"]), ("Wv", "bv", st["vg"])]):
            for sub in range(2):
                col0 = g * 512 + sub * 256
                li = (g * 3 + mi) * 2 + sub
                bb = li % 2
                if li >= 2:
                    p.wait("gpsimd", s_mt, wfree[li - 2])
                    p.wait("gpsimd", s_mv, vfree[li - 2])
                p.dma("gpsimd", st["wblk3"][bb][:],
                      dr[wname].rearrange("(kt p) o -> p kt o", p=128)[:, :, col0:col0 + 256],
                      s_w)
                p.dma("gpsimd", st["bias3"][bb][:],
                      dr[bname][col0:col0 + 256].unsqueeze(0).unsqueeze(0)
                      .broadcast_to([1, 128, 256]), s_w)
                p.wait("tensor", s_w, 32 * (li + 1))
                if li == 0:
                    p.wait("tensor", s_c, tabs_ready)
                    p.wait("vector", s_c, tabs_ready)
                for tt in range(16):
                    gi = li * 16 + tt
                    pm = st["pb"][gi % 2]
                    if gi >= 2:
                        p.wait("tensor", s_mv, pfree[gi - 2])
                    for kt in range(16):
                        p.op("tensor",
                             lambda q, kt=kt, tt=tt, bb=bb, pm=pm:
                             q.matmul(pm[:, 0:256],
                                      st["xn3T"][kt][:, tt * 128:(tt + 1) * 128],
                                      st["wblk3"][bb][:, kt * 256:(kt + 1) * 256],
                                      start=(kt == 0), stop=(kt == 15)),
                             s_mt if kt == 15 else None)
                    p.wait("vector", s_mt, s_mt.n)
                    if mi == 2:
                        p.op("vector",
                             lambda q, tt=tt, bb=bb, pm=pm, dst=dst, sub=sub:
                             q.tensor_tensor(
                                 out=dst[:, tt * 512 + sub * 256:tt * 512 + sub * 256 + 256],
                                 in0=pm[:, 0:256], in1=st["bias3"][bb][:], op=OP.add), s_mv)
                    else:
                        p.op("vector",
                             lambda q, bb=bb, pm=pm:
                             q.tensor_tensor(out=st["praw"][:], in0=pm[:, 0:256],
                                             in1=st["bias3"][bb][:], op=OP.add), s_mv)
                        for hh in range(2):
                            hloc = sub * 2 + hh
                            tab = st["qs_sb"] if mi == 0 else st["ks_sb"]
                            pr = st["praw"][:, hh * 128:(hh + 1) * 128]
                            rot = st["rot"]
                            p.op("vector",
                                 lambda q, pr=pr, rot=rot:
                                 q.tensor_scalar(out=rot[:, 0:128:2], in0=pr[:, 1:128:2],
                                                 scalar1=-1.0, scalar2=None, op0=OP.mult))
                            p.op("vector",
                                 lambda q, pr=pr, rot=rot:
                                 q.tensor_copy(out=rot[:, 1:128:2], in_=pr[:, 0:128:2]), s_rp)
                            p.op("vector",
                                 lambda q, pr=pr, tt=tt:
                                 q.tensor_tensor(out=st["rtmp1"][:], in0=pr,
                                                 in1=st["cos_sb"][:, tt * 128:(tt + 1) * 128],
                                                 op=OP.mult))
                            p.wait("vector", s_rp, s_rp.n)
                            p.op("vector",
                                 lambda q, rot=rot, tt=tt:
                                 q.tensor_tensor(out=st["rtmp2"][:], in0=rot[:],
                                                 in1=st["sin_sb"][:, tt * 128:(tt + 1) * 128],
                                                 op=OP.mult))
                            p.op("vector",
                                 lambda q: q.tensor_tensor(out=st["rtmp1"][:], in0=st["rtmp1"][:],
                                                           in1=st["rtmp2"][:], op=OP.add))
                            p.op("vector",
                                 lambda q, dst=dst, tt=tt, hloc=hloc, tab=tab:
                                 q.tensor_tensor(
                                     out=dst[:, tt * 512 + hloc * 128:tt * 512 + hloc * 128 + 128],
                                     in0=st["rtmp1"][:],
                                     in1=tab[:, tt * 128:(tt + 1) * 128], op=OP.mult),
                                 s_mv if hh == 1 else None)
                    pfree[gi] = s_mv.n
                wfree[li] = s_mt.n
                vfree[li] = s_mv.n

        if g == 3 and st.get("dbg"):
            s_dbg = p.sem("p3dbg")
            p.wait("sync", s_mv, s_mv.n)
            p.dma("sync", dr["dbg_qg"], st["qg"][:], s_dbg)
            p.dma("sync", dr["dbg_kg"], st["kg"][:], s_dbg)
            p.dma("sync", dr["dbg_vg"], st["vg"][:], s_dbg)
            for eng in ENGINES:
                p.wait(eng, s_dbg, s_dbg.n)
        for eng in ENGINES:
            p.wait(eng, s_mv, s_mv.n)
        s_tt = p.sem("p3t%d" % g)
        s_te = p.sem("p3e%d" % g)
        s_sc = p.sem("p3s%d" % g)
        s_gv = p.sem("p3v%d" % g)
        s_gs = p.sem("p3g%d" % g)
        s_gd = p.sem("p3d%d" % g)
        rfree = {}
        ufree = {}
        for hl in range(4):
            p.op("vector", lambda q, hl=hl: q.memset(st["S"][hl][0][:], 0.0))
            p.op("vector", lambda q, hl=hl: q.memset(st["S_bf"][hl][:], 0.0), s_gv)
        p.wait("tensor", s_gv, s_gv.n)
        p.wait("tensor", s_mv, s_mv.n)
        for c in range(16):
            for hl in range(4):
                gh = g * 4 + hl
                it = c * 4 + hl
                par = it % 2
                gL = float(GAMMAS[gh] ** 128)
                pbt = st["pb"][par * 4 + 0][:].bitcast(BF16)
                pbs = st["pb"][par * 4 + 1]
                pbr = st["pb"][par * 4 + 2]
                pbu = st["pb"][par * 4 + 3]
                qsl = st["qg"][:, c * 512 + hl * 128:c * 512 + hl * 128 + 128]
                ksl = st["kg"][:, c * 512 + hl * 128:c * 512 + hl * 128 + 128]
                vsl = st["vg"][:, c * 512 + hl * 128:c * 512 + hl * 128 + 128]
                if it >= 2:
                    p.wait("vector", s_tt, 4 * (it - 2) + 4)
                p.op("vector", lambda q, qsl=qsl, gh=gh, par=par:
                     q.tensor_scalar(out=st["qs_t"][par][:], in0=qsl,
                                     scalar1=st["gpow"][:, gh:gh + 1], scalar2=None,
                                     op0=OP.mult))
                p.op("vector", lambda q, ksl=ksl, gh=gh, par=par:
                     q.tensor_scalar(out=st["ku_t"][par][:], in0=ksl,
                                     scalar1=st["gsc"][:, gh:gh + 1], scalar2=None,
                                     op0=OP.mult), s_gv)
                if it >= 2:
                    p.wait("tensor", s_te, 3 * (it - 2) + 3)
                p.wait("tensor", s_gv, s_gv.n)
                p.op("tensor", lambda q, qsl=qsl, pbt=pbt:
                     q.transpose(pbt[:, 0:128], qsl, st["ident_bf"][:]))
                p.op("tensor", lambda q, par=par, pbt=pbt:
                     q.transpose(pbt[:, 128:256], st["qs_t"][par][:], st["ident_bf"][:]))
                p.op("tensor", lambda q, ksl=ksl, pbt=pbt:
                     q.transpose(pbt[:, 256:384], ksl, st["ident_bf"][:]), s_tt)
                if it >= 2:
                    p.wait("scalar", s_tt, 4 * (it - 2) + 4)
                p.wait("scalar", s_tt, s_tt.n)
                for sl_i, dst_t in [(0, st["qT_t"]), (1, st["qsT_t"]), (2, st["kT_t"])]:
                    p.op("scalar", lambda q, sl_i=sl_i, dst_t=dst_t, par=par, pbt=pbt:
                         q.copy(out=dst_t[par][:], in_=pbt[:, sl_i * 128:(sl_i + 1) * 128]),
                         s_te)
                p.wait("tensor", s_te, s_te.n)
                if it >= 2:
                    p.wait("tensor", s_sc, it - 1)
                p.op("tensor", lambda q, par=par, pbs=pbs:
                     q.matmul(pbs[:, 0:128], st["kT_t"][par][:], st["qT_t"][par][:],
                              start=True, stop=True), s_tt)
                p.wait("vector", s_tt, s_tt.n)
                p.op("vector", lambda q, gh=gh, par=par, pbs=pbs:
                     q.tensor_tensor(out=st["scT"][par][:], in0=pbs[:, 0:128],
                                     in1=st["mask"][gh][:], op=OP.mult), s_sc)
                p.wait("tensor", s_sc, s_sc.n)
                if it >= 2:
                    p.wait("tensor", s_gv, rfree[it - 2])
                if c == 0:
                    p.op("tensor", lambda q, par=par, vsl=vsl, pbr=pbr:
                         q.matmul(pbr[:, 0:128], st["scT"][par][:], vsl, start=True,
                                  stop=True), s_tt)
                else:
                    p.op("tensor", lambda q, par=par, vsl=vsl, pbr=pbr:
                         q.matmul(pbr[:, 0:128], st["scT"][par][:], vsl, start=True,
                                  stop=False))
                    p.op("tensor", lambda q, par=par, hl=hl, pbr=pbr:
                         q.matmul(pbr[:, 0:128], st["qsT_t"][par][:], st["S_bf"][hl][:],
                                  start=False, stop=True), s_tt)
                if it >= 2:
                    p.wait("tensor", s_gv, ufree[it - 2])
                p.op("tensor", lambda q, par=par, vsl=vsl, pbu=pbu:
                     q.matmul(pbu[:, 0:128], st["ku_t"][par][:], vsl, start=True, stop=True),
                     s_tt)
                p.wait("vector", s_tt, s_tt.n)
                if c >= 2 and hl == 0:
                    p.wait("vector", s_gd, 16 * (c - 1))
                p.op("vector", lambda q, pbr=pbr:
                     q.bn_stats(out=st["stats"][:, 0:6], in_=pbr[:, 0:128]), s_gv)
                p.wait("vector", s_gv, s_gv.n)
                p.op("vector", lambda q: q.bn_aggr(out=st["mv"][:], in_=st["stats"][:, 0:6]),
                     s_gv)
                p.wait("vector", s_gv, s_gv.n)
                p.op("vector", lambda q: q.tensor_scalar(out=st["negmean"][:], in0=st["mv"][:, 0:1],
                                                         scalar1=-1.0, scalar2=None, op0=OP.mult))
                p.op("vector", lambda q: q.tensor_scalar(out=st["veps"][:], in0=st["mv"][:, 1:2],
                                                         scalar1=1e-5, scalar2=None, op0=OP.add),
                     s_gv)
                p.wait("scalar", s_gv, s_gv.n)
                p.op("scalar", lambda q: q.activation(out=st["rstd"][:], in_=st["veps"][:],
                                                      func=AF.Sqrt), s_gs)
                p.wait("vector", s_gs, s_gs.n)
                p.op("vector", lambda q: q.reciprocal(out=st["rstd"][:], in_=st["rstd"][:]), s_gv)
                p.wait("vector", s_gv, s_gv.n)
                p.op("vector", lambda q, hl=hl, c=c, pbr=pbr:
                     q.tensor_scalar(out=st["gn_st"][c % 2][:, hl * 128:(hl + 1) * 128],
                                     in0=pbr[:, 0:128], scalar1=st["negmean"][:],
                                     scalar2=st["rstd"][:], op0=OP.add, op1=OP.mult), s_gv)
                rfree[it] = s_gv.n
                sold = st["S"][hl][c % 2]
                snew = st["S"][hl][(c + 1) % 2]
                p.op("vector", lambda q, sold=sold, snew=snew, gL=gL, pbu=pbu:
                     q.scalar_tensor_tensor(out=snew[:], in0=sold[:], scalar=gL,
                                            in1=pbu[:, 0:128], op0=OP.mult, op1=OP.add), s_gv)
                ufree[it] = s_gv.n
                p.wait("scalar", s_gv, s_gv.n)
                p.op("scalar", lambda q, hl=hl, snew=snew:
                     q.copy(out=st["S_bf"][hl][:], in_=snew[:]), s_gs)
                p.wait("tensor", s_gs, s_gs.n)
            p.wait("sync", s_gv, s_gv.n)
            p.dma("sync", dr["gn_hbm"][c * 128:(c + 1) * 128, g * 512:(g + 1) * 512],
                  st["gn_st"][c % 2][:], s_gd)
        p.wait("vector", s_gd, s_gd.n)
        p.wait("sync", s_gd, s_gd.n)
        p.op("sync", lambda q: q.nop(), s_gall)
        p.wait("tensor", s_te, s_te.n)
        p.wait("tensor", s_sc, s_sc.n)
        p.wait("tensor", s_gv, s_gv.n)

    s_dw = p.sem("p3dw")
    s_dt = p.sem("p3dt")
    s_dv = p.sem("p3dv")
    s_ds = p.sem("p3ds")
    dwfree = {}
    dpfree = {}
    p.wait("gpsimd", s_mt, s_mt.n)
    for sub in range(8):
        bb = sub % 2
        if sub >= 2:
            p.wait("gpsimd", s_dt, dwfree[sub - 2])
            p.wait("gpsimd", s_ds, (sub - 2) * 16 + 16)
        p.dma("gpsimd", st["wblk3"][bb][:],
              dr["Wg"].rearrange("(kt p) o -> p kt o", p=128)[:, :, sub * 256:(sub + 1) * 256],
              s_dw)
        p.dma("gpsimd", st["bias3"][bb][:],
              dr["bg"][sub * 256:(sub + 1) * 256].unsqueeze(0).unsqueeze(0)
              .broadcast_to([1, 128, 256]), s_dw)
        p.wait("tensor", s_dw, 32 * (sub + 1))
        for tt in range(16):
            gi = sub * 16 + tt
            pm = st["pb"][gi % 2]
            if gi >= 2:
                p.wait("tensor", s_dv, dpfree[gi - 2])
            for kt in range(16):
                p.op("tensor",
                     lambda q, kt=kt, tt=tt, bb=bb, pm=pm:
                     q.matmul(pm[:, 0:256], st["xn3T"][kt][:, tt * 128:(tt + 1) * 128],
                              st["wblk3"][bb][:, kt * 256:(kt + 1) * 256],
                              start=(kt == 0), stop=(kt == 15)),
                     s_dt if kt == 15 else None)
            p.wait("vector", s_dt, s_dt.n)
            p.op("vector",
                 lambda q, bb=bb, pm=pm: q.tensor_tensor(out=st["praw"][:], in0=pm[:, 0:256],
                                                         in1=st["bias3"][bb][:], op=OP.add),
                 s_dv)
            dpfree[gi] = s_dv.n
            p.wait("scalar", s_dv, s_dv.n)
            p.op("scalar",
                 lambda q, sub=sub, tt=tt:
                 q.activation(out=gsil_view(st, tt)[:, sub * 256:(sub + 1) * 256],
                              in_=st["praw"][:], func=AF.Silu), s_ds)
        dwfree[sub] = s_dt.n
    p.wait("tensor", s_ds, s_ds.n)
    p.wait("sync", s_ds, s_ds.n)

    s_gn = p.sem("p3gn")
    s_ov = p.sem("p3ov")
    mill2 = TransposeMill(p, st["pb"][6:8], "p3om", BF16, "scalar")
    p.wait("sync", s_gall, 4)
    for tt in range(16):
        b = tt % 2
        if tt >= 2:
            p.wait("sync", s_ov, (tt - 2) + 1)
        p.dma("sync", st["gn_in"][b], dr["gn_hbm"][tt * 128:(tt + 1) * 128, :], s_gn)
        p.wait("vector", s_gn, 16 * (tt + 1))
        if tt >= 2:
            p.wait("vector", mill2.s_grp, (tt - 2) * 2 + 2)
        p.op("vector", lambda q, tt=tt, b=b:
             q.tensor_tensor(out=st["oin"][b], in0=gsil_view(st, tt)[:], in1=st["gn_in"][b],
                             op=OP.mult), s_ov)
        p.wait("tensor", s_ov, s_ov.n)

        def src_fn(dt, b=b):
            return st["oin"][b][:, dt * 128:(dt + 1) * 128]

        def evac_fn(dt, pv, tt=tt):
            p.op("scalar",
                 lambda q, dt=dt, pv=pv, tt=tt:
                 q.copy(out=st["xn3T"][dt][:, tt * 128:(tt + 1) * 128], in_=pv),
                 mill2.s_done)

        mill2.go(src_fn, evac_fn, ND, st["ident_bf"][:])
    for eng in ENGINES:
        p.wait(eng, mill2.s_done, mill2.s_done.n)
    if st.get("dbg"):
        s_dbg2 = p.sem("p3dbg2")
        p.wait("sync", mill2.s_done, mill2.s_done.n)
        for dt in range(ND):
            p.dma("sync", dr["dbg_oinT"][dt * 128:(dt + 1) * 128, :], st["xn3T"][dt][:], s_dbg2)
        for tt2 in range(16):
            p.dma("sync", dr["dbg_gsil"][tt2 * 128:(tt2 + 1) * 128, :],
                  gsil_view(st, tt2)[:] if hasattr(gsil_view(st, tt2), 'ap') else gsil_view(st, tt2),
                  s_dbg2)
        for eng in ENGINES:
            p.wait(eng, s_dbg2, s_dbg2.n)

    s_ww = p.sem("p3ww")
    s_xc = p.sem("p3xc")
    s_wt = p.sem("p3wt")
    s_wv = p.sem("p3wv")
    s_wst = p.sem("p3wst")
    w3free = {}
    p3free = {}
    for sub in range(8):
        bb = sub % 2
        if sub >= 2:
            p.wait("gpsimd", s_wt, w3free[sub - 2])
            p.wait("gpsimd", s_wv, (sub - 2) * 16 + 16)
        p.dma("gpsimd", st["wblk3"][bb][:],
              dr["Wo"].rearrange("(kt p) o -> p kt o", p=128)[:, :, sub * 256:(sub + 1) * 256],
              s_ww)
        p.dma("gpsimd", st["bias3"][bb][:],
              dr["bo"][sub * 256:(sub + 1) * 256].unsqueeze(0).unsqueeze(0)
              .broadcast_to([1, 128, 256]), s_ww)
        for tt in range(16):
            gi = sub * 16 + tt
            if gi >= 2:
                p.wait("sync", s_wv, p3free[gi - 2])
            p.dma("sync", st["xc2col"][tt % 2][:],
                  dr["xc2"][tt * 128:(tt + 1) * 128, sub * 256:(sub + 1) * 256], s_xc)
            pm = st["pb"][gi % 2]
            p.wait("tensor", s_ww, 32 * (sub + 1))
            p.wait("tensor", s_xc, 16 * (gi + 1))
            if gi >= 2:
                p.wait("tensor", s_wv, p3free[gi - 2])
            for kt in range(16):
                p.op("tensor",
                     lambda q, kt=kt, tt=tt, bb=bb, pm=pm:
                     q.matmul(pm[:, 0:256], st["xn3T"][kt][:, tt * 128:(tt + 1) * 128],
                              st["wblk3"][bb][:, kt * 256:(kt + 1) * 256],
                              start=(kt == 0), stop=(kt == 15)),
                     s_wt if kt == 15 else None)
            p.wait("vector", s_wt, s_wt.n)
            p.op("vector",
                 lambda q, bb=bb, pm=pm: q.tensor_tensor(out=st["praw"][:], in0=pm[:, 0:256],
                                                         in1=st["bias3"][bb][:], op=OP.add))
            if gi >= 2:
                p.wait("vector", s_wst, 16 * (gi - 1))
            p.op("vector",
                 lambda q, tt=tt: q.tensor_tensor(out=st["outcol"][tt % 2][:], in0=st["praw"][:],
                                                  in1=st["xc2col"][tt % 2][:], op=OP.add), s_wv)
            p3free[gi] = s_wv.n
            p.wait("sync", s_wv, s_wv.n)
            p.dma("sync", dr["xc3"][tt * 128:(tt + 1) * 128, sub * 256:(sub + 1) * 256],
                  st["outcol"][tt % 2][:], s_wst)
        w3free[sub] = s_wt.n
    p.wait("sync", s_wst, s_wst.n)


def gsil_view(st, tt):
    if tt < 4:
        return st["qg"][:, tt * 2048:(tt + 1) * 2048]
    if tt < 8:
        return st["kg"][:, (tt - 4) * 2048:(tt - 3) * 2048]
    if tt < 12:
        return st["vg"][:, (tt - 8) * 2048:(tt - 7) * 2048]
    tab = [st["cos_sb"], st["sin_sb"], st["qs_sb"], st["ks_sb"]][tt - 12]
    return tab[:].bitcast(BF16)


def alloc_p3(nc, stack, st):
    sb = lambda name, shape, dt: stack.enter_context(nc.sbuf_tensor(name, shape, dt))
    st["xn3T"] = [sb("xn3T_%d" % k, [128, T], BF16) for k in range(ND)]
    st["qg"] = sb("qg", [128, 16 * 512], BF16)
    st["kg"] = sb("kg", [128, 16 * 512], BF16)
    st["vg"] = sb("vg", [128, 16 * 512], BF16)
    st["cos_sb"] = sb("cos_sb", [128, 16 * 128], F16)
    st["sin_sb"] = sb("sin_sb", [128, 16 * 128], F16)
    st["qs_sb"] = sb("qs_sb", [128, 16 * 128], F16)
    st["ks_sb"] = sb("ks_sb", [128, 16 * 128], F16)
    st["wblk3"] = [sb("wblk3_%d" % k, [128, 16 * 256], BF16) for k in range(2)]
    st["bias3"] = [sb("bias3_%d" % k, [128, 256], F32) for k in range(2)]
    st["praw"] = sb("praw", [128, 256], F32)
    st["rot"] = sb("rot", [128, 128], F32)
    st["rtmp1"] = sb("rtmp1", [128, 128], F32)
    st["rtmp2"] = sb("rtmp2", [128, 128], F32)
    st["mask"] = [sb("mask%d" % k, [128, 128], BF16) for k in range(H)]
    st["gpow"] = sb("gpow", [128, H], F32)
    st["gsc"] = sb("gsc", [128, H], F32)
    st["S"] = [[sb("S%d_%d" % (k, j), [128, 128], F32) for j in range(2)] for k in range(4)]
    st["S_bf"] = [sb("Sbf%d" % k, [128, 128], BF16) for k in range(4)]
    for nm in ("qT_t", "qsT_t", "kT_t", "qs_t", "ku_t", "scT"):
        st[nm] = [sb(nm + "%d" % k, [128, 128], BF16) for k in range(2)]
    st["gn_st"] = [sb("gn_st%d" % k, [128, 512], BF16) for k in range(2)]
    st["xc2col"] = [sb("xc2col%d" % k, [128, 256], F32) for k in range(2)]
    st["outcol"] = [sb("outcol%d" % k, [128, 256], F32) for k in range(2)]
    st["gn_in"] = [st["wblk3"][0][:, 0:2048], st["wblk3"][1][:, 0:2048]]
    st["oin"] = [st["wblk3"][0][:, 2048:4096], st["wblk3"][1][:, 2048:4096]]


def build_p3(standalone=True, dbg=False):
    nc = bass.Bass("TRN2", target_bir_lowering=False)
    dr = {}
    ext_in = lambda n, s, d: nc.dram_tensor(n, s, d, kind="ExternalInput").ap()
    dr["xc2"] = ext_in("xc2", [T, D], F32)
    for nm in ("Wq", "Wk", "Wv", "Wg", "Wo"):
        dr[nm] = ext_in(nm, [D, D], F32)
    for nm in ("bq", "bk", "bv", "bg", "bo"):
        dr[nm] = ext_in(nm, [D], F32)
    dr["cos_h"] = ext_in("cos_h", [T, KD], F32)
    dr["sin_h"] = ext_in("sin_h", [T, KD], F32)
    dr["qs_h"] = ext_in("qs_h", [T, KD], F32)
    dr["ks_h"] = ext_in("ks_h", [T, KD], F32)
    dr["masks_h"] = ext_in("masks_h", [H, 128, 128], F32)
    dr["gpow_h"] = ext_in("gpow_h", [128, H], F32)
    dr["gsc_h"] = ext_in("gsc_h", [128, H], F32)
    dr["ident_bf_h"] = ext_in("ident_bf_h", [128, 128], BF16)
    dr["ident_f32_h"] = ext_in("ident_f32_h", [128, 128], F32)
    dr["gn_hbm"] = nc.dram_tensor("gn_hbm", [T, D], BF16, kind="ExternalOutput").ap()
    dr["xc3"] = nc.dram_tensor("xc3", [T, D], F32, kind="ExternalOutput").ap()
    if dbg:
        dr["dbg_qg"] = nc.dram_tensor("dbg_qg", [128, 8192], BF16, kind="ExternalOutput").ap()
        dr["dbg_kg"] = nc.dram_tensor("dbg_kg", [128, 8192], BF16, kind="ExternalOutput").ap()
        dr["dbg_vg"] = nc.dram_tensor("dbg_vg", [128, 8192], BF16, kind="ExternalOutput").ap()
        dr["dbg_oinT"] = nc.dram_tensor("dbg_oinT", [D, T], BF16, kind="ExternalOutput").ap()
        dr["dbg_gsil"] = nc.dram_tensor("dbg_gsil", [T, D], BF16, kind="ExternalOutput").ap()
    with contextlib.ExitStack() as stack:
        st = {}
        alloc_common(nc, stack, st)
        alloc_p3(nc, stack, st)
        p = Prog(nc)
        st["dbg"] = dbg
        s_c = p.sem("const")
        p.dma("sync", st["ident_bf"][:], dr["ident_bf_h"], s_c)
        p.dma("sync", st["ident_f32"][:], dr["ident_f32_h"], s_c)
        p.wait("tensor", s_c, 32)
        emit_p3(p, st, dr)
        assert p.check_deadlock()
        p.finalize(stack)
    return nc


def _p3_consts():
    S, dim, theta, sbase_den = T, KD, 10000.0, 1.4 * KD
    pos = np.arange(S, dtype=np.float64)
    inv_freq = 1.0 / theta ** (np.arange(0, dim, 2, dtype=np.float64) / dim)
    freqs = np.repeat(pos[:, None] * inv_freq[None, :], 2, axis=-1)
    cos = np.cos(freqs).astype(np.float32)
    sin = np.sin(freqs).astype(np.float32)
    sbase = (np.arange(0, dim, 2, dtype=np.float64) + 0.4 * dim) / sbase_den
    power = (pos - S // 2) / 512.0
    scale = np.repeat(sbase[None, :] ** power[:, None], 2, axis=-1)
    qs = scale.astype(np.float32)
    ks = (1.0 / scale).astype(np.float32)
    idx = np.arange(128, dtype=np.float64)
    pm = idx[:, None] - idx[None, :]
    masks = np.zeros((H, 128, 128), np.float32)
    gpow = np.zeros((128, H), np.float32)
    gsc = np.zeros((128, H), np.float32)
    for h in range(H):
        gm = GAMMAS[h]
        masks[h] = np.where(pm >= 0, gm ** np.maximum(pm, 0.0), 0.0).T.astype(np.float32)
        gpow[:, h] = (gm ** (idx + 1)).astype(np.float32)
        gsc[:, h] = (gm ** (127 - idx)).astype(np.float32)
    return {"cos_h": cos, "sin_h": sin, "qs_h": qs, "ks_h": ks,
            "masks_h": masks, "gpow_h": gpow, "gsc_h": gsc}


def _p3_prep(inp):
    d = {
        "Wq": np.asarray(inp["L1_attn_Wq"]), "bq": np.asarray(inp["L1_attn_bq"]),
        "Wk": np.asarray(inp["L1_attn_Wk"]), "bk": np.asarray(inp["L1_attn_bk"]),
        "Wv": np.asarray(inp["L1_attn_Wv"]), "bv": np.asarray(inp["L1_attn_bv"]),
        "Wg": np.asarray(inp["L1_attn_Wg"]), "bg": np.asarray(inp["L1_attn_bg"]),
        "Wo": np.asarray(inp["L1_attn_Wo"]), "bo": np.asarray(inp["L1_attn_bo"]),
        "ident_bf_h": _IDENT_BF, "ident_f32_h": _IDENT_F32,
    }
    d.update(_p3_consts())
    return d


def emit_dequant(p, st, w_i32, scale_f32, out_bf, j, tag_sems):
    s_v, s_s = tag_sems
    t1, t2, t3 = st["dq1"], st["dq2"], st["dq3"]
    ft = st["dqf"]
    P = w_i32.shape[0]
    V = lambda fn, sem=None: p.op("vector", fn, sem)
    V(lambda q: q.tensor_scalar(out=t1[0:P, :], in0=w_i32, scalar1=4 * j, scalar2=15,
                                op0=OP.logical_shift_right, op1=OP.bitwise_and))
    V(lambda q: q.tensor_scalar(out=t2[0:P, :], in0=t1[0:P, :], scalar1=7, scalar2=2,
                                op0=OP.bitwise_and, op1=OP.logical_shift_left))
    V(lambda q: q.tensor_scalar(out=t3[0:P, :], in0=t2[0:P, :], scalar1=48, scalar2=None,
                                op0=OP.add))
    V(lambda q: q.tensor_scalar(out=t2[0:P, :], in0=t2[0:P, :], scalar1=12, scalar2=None,
                                op0=OP.mult))
    V(lambda q: q.tensor_tensor(out=t3[0:P, :], in0=t3[0:P, :], in1=t2[0:P, :], op=OP.min))
    V(lambda q: q.tensor_scalar(out=t2[0:P, :], in0=t1[0:P, :], scalar1=8, scalar2=4,
                                op0=OP.bitwise_and, op1=OP.logical_shift_left))
    V(lambda q: q.tensor_tensor(out=t1[0:P, :], in0=t3[0:P, :], in1=t2[0:P, :], op=OP.add),
      s_v)
    p.wait("scalar", s_v, s_v.n)
    p.op("scalar", lambda q: q.copy(out=ft[0:P, :],
                                    in_=t1[0:P, :].bitcast(FP8)[:, 0:2048:4]), s_s)
    p.wait("vector", s_s, s_s.n)
    V(lambda q: q.tensor_tensor(out=out_bf, in0=ft[0:P, :], in1=scale_f32, op=OP.mult), s_v)


def emit_p4(p, st, dr):
    lnb = make_ln_bundle(p, st, "p4ln")

    def col_sel(dt):
        P, j = dt // 8, dt % 8
        return (1024 * P + j, 8)

    s_ln = emit_ln_transpose(p, st, dr["xc3"], st["xn4T"], "p4a", bundle=lnb,
                             col_sel=col_sel)
    p.wait("tensor", s_ln, s_ln.n)
    p.wait("vector", s_ln, s_ln.n)

    s_pk = p.sem("p4pk")
    s_sc = p.sem("p4sc")
    s_dv = p.sem("p4dv")
    s_ds = p.sem("p4ds")
    s_mt = p.sem("p4mt")
    s_sg = p.sem("p4sg")
    s_hv = p.sem("p4hv")
    s_tt = p.sem("p4tt")
    s_te = p.sem("p4te")
    s_hd = p.sem("p4hd")
    SUPW = [1024] * 5 + [512]

    def load_scale(dst, rows0, cols, sem):
        p.dma("sync", st["sc16"][:],
              dr["gup_scales"][rows0:rows0 + 8, cols:cols + 512].unsqueeze(1)
              .broadcast_to([8, 16, 512]), sem)

    ob_idx = 0
    for sup in range(6):
        w = SUPW[sup]
        for ib in range(w // 512):
            ocol = sup * 1024 + ib * 512
            bb = ob_idx % 2
            if ob_idx >= 2:
                p.wait("gpsimd", s_dv, st["dqfree"][ob_idx - 2])
                p.wait("sync", s_dv, st["dqfree"][ob_idx - 2])
            for pt in range(2):
                p.dma("gpsimd", st["pkg"][bb][pt][:],
                      dr["gup_pT"][128 * pt:128 * (pt + 1), ocol:ocol + 512], s_pk)
                p.dma("gpsimd", st["pku"][bb][pt][:],
                      dr["gup_pT"][128 * pt:128 * (pt + 1), I_FF + ocol:I_FF + ocol + 512],
                      s_pk)
                p.dma("sync", st["scg16"][bb][pt][:],
                      dr["gup_scales"][8 * pt:8 * pt + 8, ocol:ocol + 512].unsqueeze(1)
                      .broadcast_to([8, 16, 512]), s_sc)
                p.dma("sync", st["scu16"][bb][pt][:],
                      dr["gup_scales"][8 * pt:8 * pt + 8, I_FF + ocol:I_FF + ocol + 512]
                      .unsqueeze(1).broadcast_to([8, 16, 512]), s_sc)
            p.wait("scalar", s_sc, s_sc.n)
            if ob_idx >= 1:
                p.wait("scalar", s_dv, st["dqfree"][ob_idx - 1])
            for pt in range(2):
                p.op("scalar", lambda q, bb=bb, pt=pt: q.copy(out=st["scg"][pt][:],
                                                              in_=st["scg16"][bb][pt][:]))
                p.op("scalar", lambda q, bb=bb, pt=pt: q.copy(out=st["scu"][pt][:],
                                                              in_=st["scu16"][bb][pt][:]),
                     s_ds)
            p.wait("vector", s_ds, s_ds.n)
            p.wait("vector", s_pk, s_pk.n)
            if ob_idx >= 1:
                p.wait("vector", s_mt, st["mmfree"][ob_idx - 1])
                p.wait("scalar", s_mt, st["mmfree"][ob_idx - 1])
            for pt in range(2):
                for j in range(8):
                    kt = pt * 8 + j
                    emit_dequant(p, st, st["pkg"][bb][pt][:], st["scg"][pt][:],
                                 st["wgate"][kt][:], j, (s_dv, s_ds))
                    emit_dequant(p, st, st["pku"][bb][pt][:], st["scu"][pt][:],
                                 st["wup"][kt][:], j, (s_dv, s_ds))
            st["dqfree"][ob_idx] = s_dv.n
            p.wait("tensor", s_dv, s_dv.n)
            for tt in range(16):
                pg = st["pb"][tt % 2]
                pu = st["pb"][2 + tt % 2]
                if ob_idx * 16 + tt >= 2:
                    p.wait("tensor", s_hv, st["hfree"][ob_idx * 16 + tt - 2])
                for kt in range(16):
                    p.op("tensor",
                         lambda q, kt=kt, tt=tt, pg=pg:
                         q.matmul(pg[:], st["xn4T"][kt][:, tt * 128:(tt + 1) * 128],
                                  st["wgate"][kt][:], start=(kt == 0), stop=(kt == 15)))
                for kt in range(16):
                    p.op("tensor",
                         lambda q, kt=kt, tt=tt, pu=pu:
                         q.matmul(pu[:], st["xn4T"][kt][:, tt * 128:(tt + 1) * 128],
                                  st["wup"][kt][:], start=(kt == 0), stop=(kt == 15)),
                         s_mt if kt == 15 else None)
                p.wait("scalar", s_mt, s_mt.n)
                if ob_idx * 16 + tt >= 2:
                    p.wait("scalar", s_hv, st["hfree"][ob_idx * 16 + tt - 2])
                p.op("scalar", lambda q, tt=tt, pg=pg:
                     q.activation(out=st["sg_sb"][tt % 2][:], in_=pg[:], func=AF.Silu), s_sg)
                p.wait("vector", s_sg, s_sg.n)
                p.op("vector", lambda q, tt=tt, ib=ib, pu=pu:
                     q.tensor_tensor(out=st["h_super"][tt][:, ib * 512:(ib + 1) * 512],
                                     in0=st["sg_sb"][tt % 2][:], in1=pu[:], op=OP.mult), s_hv)
                st["hfree"][ob_idx * 16 + tt] = s_hv.n
            st["mmfree"][ob_idx] = s_mt.n
            ob_idx += 1
        p.wait("tensor", s_hv, s_hv.n)
        rows = 128 if w == 1024 else 64
        for j in range(8):
            jb = j % 2
            pbj = st["pb"][4 + jb][:].bitcast(BF16)
            gj = sup * 8 + j
            if gj >= 2:
                p.wait("tensor", s_hd, 16 * (gj - 1))
            for tt in range(16):
                p.op("tensor",
                     lambda q, j=j, tt=tt, rows=rows, pbj=pbj:
                     q.transpose(pbj[0:rows, tt % 4 * 128:(tt % 4 + 1) * 128],
                                 st["h_super"][tt][:, j:j + 8 * (rows - 1) + 1:8],
                                 st["ident_bf"][:]),
                     s_tt)
                p.wait("scalar", s_tt, s_tt.n)
                p.op("scalar",
                     lambda q, j=j, tt=tt, rows=rows, jb=jb, pbj=pbj:
                     q.copy(out=st["hT_st"][jb][0:rows, tt * 128:(tt + 1) * 128],
                            in_=pbj[0:rows, tt % 4 * 128:(tt % 4 + 1) * 128]), s_te)
                p.wait("tensor", s_te, s_te.n)
            p.wait("sync", s_te, s_te.n)
            r0 = sup * 1024 + j * rows
            p.dma("sync", dr["h_actT_hbm"][r0:r0 + rows, :], st["hT_st"][jb][0:rows, :], s_hd)
        p.wait("vector", s_te, s_te.n)
    p.wait("sync", s_hd, s_hd.n)

    s_hl = p.sem("p4hl")
    s_dn = p.sem("p4dn")
    s_ct = p.sem("p4ct")
    s_cv = p.sem("p4cv")
    s_cs = p.sem("p4cs")
    s_fo = p.sem("p4fo")
    NKT = 48
    p.wait("tensor", s_hv, s_hv.n)
    p.wait("tensor", s_sg, s_sg.n)
    p.wait("tensor", s_te, s_te.n)
    for half in range(2):
        if half == 1:
            for eng in ENGINES:
                p.wait(eng, s_fo, 16 * 8)
            p.wait("sync", s_dn, s_dn.n)
        t0 = half * 1024
        for db in range(4):
            for kt in range(NKT):
                s5 = kt >= 40
                rows = 64 if s5 else 128
                sup = 5 if s5 else kt // 8
                j = kt % 8
                r0 = sup * 1024 + j * rows
                gk = (half * 4 + db) * NKT + kt
                hb = gk % 3
                if gk >= 3:
                    p.wait("sync", s_ct, st["hlfree"].get(gk - 3, 0))
                p.dma("sync", st["hT_in"][hb][0:rows, :],
                      dr["h_actT_hbm"][r0:r0 + rows, t0:t0 + 1024], s_hl)
                wb = gk % 2
                if half == 0:
                    if j == 0:
                        p.wait("gpsimd", s_dv, s_dv.n)
                        p.dma("gpsimd", st["pkd"][0:rows, :],
                              dr["down_pT"][128 * sup:128 * sup + rows,
                                            db * 512:(db + 1) * 512], s_pk)
                        p.dma("sync", st["scd16"][0:rows, :],
                              dr["down_scales"][8 * sup:8 * sup + (8 if not s5 else 4),
                                                db * 512:(db + 1) * 512].unsqueeze(1)
                              .broadcast_to([8 if not s5 else 4, 16, 512]), s_sc)
                        p.wait("scalar", s_sc, s_sc.n)
                        p.wait("scalar", s_pk, s_pk.n)
                        p.wait("scalar", s_dv, s_dv.n)
                        p.op("scalar", lambda q, rows=rows: q.copy(out=st["scd"][0:rows, :],
                                                                   in_=st["scd16"][0:rows, :]),
                             s_ds)
                        p.wait("vector", s_ds, s_ds.n)
                    if gk >= 2:
                        p.wait("vector", s_ct, st["wfree"].get(gk - 2, 0))
                        p.wait("scalar", s_ct, st["wfree"].get(gk - 2, 0))
                        p.wait("vector", s_dn, st["wstore"].get(gk - 2, 0))
                    p.wait("vector", s_pk, s_pk.n)
                    emit_dequant(p, st, st["pkd"][0:rows, :], st["scd"][0:rows, :],
                                 st["w_dn"][wb][0:rows, :], j, (s_dv, s_ds))
                    p.wait("sync", s_dv, s_dv.n)
                    p.dma("sync", dr["w_dn_hbm"][r0:r0 + rows, db * 512:(db + 1) * 512],
                          st["w_dn"][wb][0:rows, :], s_dn)
                    st["wstore"][gk] = s_dn.n
                    p.wait("tensor", s_dv, s_dv.n)
                else:
                    if gk >= 2:
                        p.wait("sync", s_ct, st["wfree"].get(gk - 2, 0))
                    p.dma("sync", st["w_dn"][wb][0:rows, :],
                          dr["w_dn_hbm"][r0:r0 + rows, db * 512:(db + 1) * 512], s_dn)
                    p.wait("tensor", s_dn, s_dn.n)
                p.wait("tensor", s_hl, s_hl.n)
                for tt in range(8):
                    p.op("tensor",
                         lambda q, kt=kt, tt=tt, rows=rows, wb=wb, hb=hb:
                         q.matmul(st["pb"][tt][:],
                                  st["hT_in"][hb][0:rows, tt * 128:(tt + 1) * 128],
                                  st["w_dn"][wb][0:rows, :],
                                  start=(kt == 0), stop=(kt == NKT - 1)),
                         s_ct if tt == 7 else None)
                st["wfree"][gk] = s_ct.n
                st["hlfree"][gk] = s_ct.n
            p.wait("scalar", s_ct, s_ct.n)
            for tt in range(8):
                p.op("scalar", lambda q, tt=tt, db=db:
                     q.activation(out=st["sqdump"][:], in_=st["pb"][tt][:], func=AF.Square,
                                  accum_out=st["ssq"][:, (tt * 4 + db):(tt * 4 + db) + 1]))
                p.op("scalar", lambda q, tt=tt, half=half, db=db:
                     q.copy(out=st["xn4T"][half * 8 + tt][:, db * 512:(db + 1) * 512],
                            in_=st["pb"][tt][:]), s_cs)
            p.wait("tensor", s_cs, s_cs.n)
        for tt in range(8):
            gt = half * 8 + tt
            p.wait("vector", s_cs, s_cs.n)
            p.op("vector", lambda q, tt=tt: q.reduce_sum(out=st["red1"][:],
                                                         in_=st["ssq"][:, tt * 4:tt * 4 + 4],
                                                         axis=mybir.AxisListType.X), s_cv)
            p.wait("vector", s_cv, s_cv.n)
            p.op("vector", lambda q: q.tensor_scalar(out=st["red1"][:], in0=st["red1"][:],
                                                     scalar1=1.0 / 2048.0, scalar2=1e-6,
                                                     op0=OP.mult, op1=OP.add), s_cv)
            p.wait("scalar", s_cv, s_cv.n)
            p.op("scalar", lambda q: q.activation(out=st["red2"][:], in_=st["red1"][:],
                                                  func=AF.Sqrt), s_cs)
            p.wait("vector", s_cs, s_cs.n)
            p.op("vector", lambda q: q.reciprocal(out=st["red2"][:], in_=st["red2"][:]), s_cv)
            p.wait("vector", s_cv, s_cv.n)
            b = gt % 2
            if gt >= 2:
                p.wait("sync", s_cv, st["fofree"].get(gt - 2, 0))
            p.dma("sync", st["xc3t"][b][:], dr["xc3"][gt * 128:(gt + 1) * 128, :], s_hl)
            p.wait("vector", s_hl, s_hl.n)
            if gt >= 2:
                p.wait("vector", s_fo, 16 * (gt - 1))
            p.op("vector", lambda q, gt=gt, b=b:
                 q.scalar_tensor_tensor(out=st["fout"][b][:], in0=st["xn4T"][gt][:],
                                        scalar=st["red2"][:], in1=st["xc3t"][b][:],
                                        op0=OP.mult, op1=OP.add), s_cv)
            st["fofree"][gt] = s_cv.n
            p.wait("sync", s_cv, s_cv.n)
            p.dma("sync", dr["out"][gt * 128:(gt + 1) * 128, :], st["fout"][b][:], s_fo)
    p.wait("sync", s_fo, s_fo.n)


def alloc_p4(nc, stack, st):
    sb = lambda name, shape, dt: stack.enter_context(nc.sbuf_tensor(name, shape, dt))
    st["xn4T"] = [sb("xn4T_%d" % k, [128, T], BF16) for k in range(ND)]
    st["wgate"] = [sb("wg4_%d" % k, [128, 512], BF16) for k in range(16)]
    st["wup"] = [sb("wu4_%d" % k, [128, 512], BF16) for k in range(16)]
    st["hT_st"] = [st["xtile"][k][:].bitcast(BF16)[:, 0:T] for k in range(2)]
    st["pkg"] = [[sb("pkg%d_%d" % (b, pt), [128, 512], I32) for pt in range(2)]
                 for b in range(2)]
    st["pku"] = [[sb("pku%d_%d" % (b, pt), [128, 512], I32) for pt in range(2)]
                 for b in range(2)]
    st["scg16"] = [[sb("scg16_%d_%d" % (b, pt), [128, 512], F16) for pt in range(2)]
                   for b in range(2)]
    st["scu16"] = [[sb("scu16_%d_%d" % (b, pt), [128, 512], F16) for pt in range(2)]
                   for b in range(2)]
    st["scg"] = [sb("scg%d" % pt, [128, 512], F32) for pt in range(2)]
    st["scu"] = [sb("scu%d" % pt, [128, 512], F32) for pt in range(2)]
    st["sc16"] = sb("sc16", [128, 512], F16)
    st["dq1"] = sb("dq1", [128, 512], I32)
    st["dq2"] = sb("dq2", [128, 512], I32)
    st["dq3"] = sb("dq3", [128, 512], I32)
    st["dqf"] = sb("dqf", [128, 512], F32)
    st["sg_sb"] = [sb("sgsb%d" % k, [128, 512], BF16) for k in range(2)]
    st["pkd"] = sb("pkd", [128, 512], I32)
    st["scd16"] = sb("scd16", [128, 512], F16)
    st["scd"] = sb("scd", [128, 512], F32)
    st["w_dn"] = [sb("w_dn%d" % k, [128, 512], BF16) for k in range(2)]
    st["hT_in"] = [st["xnbf"][0][:][:, 0:1024], st["xnbf"][0][:][:, 1024:2048],
                   st["xnbf"][1][:][:, 0:1024]]
    st["sqdump"] = sb("sqdump", [128, 512], F32)
    st["ssq"] = sb("ssq", [128, 32], F32)
    st["red1"] = sb("red1", [128, 1], F32)
    st["red2"] = sb("red2", [128, 1], F32)
    st["xc3t"] = [sb("xc3t%d" % k, [128, D], F32) for k in range(2)]
    st["fout"] = [sb("fout%d" % k, [128, D], F32) for k in range(2)]
    hosts = [st["xc3t"][0], st["xc3t"][1], st["fout"][0], st["fout"][1]]
    st["h_super"] = [hosts[k // 4][:].bitcast(BF16)[:, (k % 4) * 1024:(k % 4 + 1) * 1024]
                     for k in range(16)]
    st["dqfree"] = {}
    st["mmfree"] = {}
    st["hfree"] = {}
    st["wfree"] = {}
    st["hlfree"] = {}
    st["wstore"] = {}
    st["fofree"] = {}


def build_p4(standalone=True, dbg=False):
    nc = bass.Bass("TRN2", target_bir_lowering=False)
    dr = {}
    ext_in = lambda n, s, d: nc.dram_tensor(n, s, d, kind="ExternalInput").ap()
    dr["xc3"] = ext_in("xc3", [T, D], F32)
    dr["gup_pT"] = ext_in("gup_pT", [256, 2 * I_FF], I32)
    dr["gup_scales"] = ext_in("gup_scales", [16, 2 * I_FF], F16)
    dr["down_pT"] = ext_in("down_pT", [704, D], I32)
    dr["down_scales"] = ext_in("down_scales", [44, D], F16)
    dr["ident_bf_h"] = ext_in("ident_bf_h", [128, 128], BF16)
    dr["ident_f32_h"] = ext_in("ident_f32_h", [128, 128], F32)
    knd = "ExternalOutput" if dbg else "Internal"
    dr["h_actT_hbm"] = nc.dram_tensor("h_actT_hbm", [I_FF, T], BF16, kind=knd).ap()
    dr["w_dn_hbm"] = nc.dram_tensor("w_dn_hbm", [I_FF, D], BF16, kind=knd).ap()
    dr["out"] = nc.dram_tensor("out", [T, D], F32, kind="ExternalOutput").ap()
    with contextlib.ExitStack() as stack:
        st = {}
        alloc_common(nc, stack, st)
        alloc_p4(nc, stack, st)
        p = Prog(nc)
        s_c = p.sem("const")
        p.dma("sync", st["ident_bf"][:], dr["ident_bf_h"], s_c)
        p.dma("sync", st["ident_f32"][:], dr["ident_f32_h"], s_c)
        p.wait("tensor", s_c, 32)
        emit_p4(p, st, dr)
        assert p.check_deadlock()
        p.finalize(stack)
    return nc


def _p4_prep(inp):
    return {
        "gup_pT": np.ascontiguousarray(np.asarray(inp["L1_ffn_gate_up_packed"]).T),
        "gup_scales": np.asarray(inp["L1_ffn_gate_up_scales"]),
        "down_pT": np.ascontiguousarray(np.asarray(inp["L1_ffn_down_packed"]).T),
        "down_scales": np.asarray(inp["L1_ffn_down_scales"]),
        "ident_bf_h": _IDENT_BF,
        "ident_f32_h": _IDENT_F32,
    }
